# version 05
# baseline (speedup 1.0000x reference)
import numpy as np
import concourse.bass as bass
import concourse.mybir as mybir
from concourse.bass_utils import run_bass_kernel_spmd

F32 = mybir.dt.float32
AF = mybir.ActivationFunctionType
ALU = mybir.AluOpType
AX = mybir.AxisListType

EPS_RMS = float(np.finfo(np.float64).eps)
EPS_LN = 1e-5
SMX = 1.0 / 12.0
E = 2048
C = 512
HID = 1024
NEXP = 8
FC = E // 128
CCH = C // 128
HCH = HID // 128
TOK = 512
QC = (C + E) // 128
ENGS = ("sync", "gpsimd", "tensor", "vector", "scalar")


class Sem:
    __slots__ = ("h", "n", "step")

    def __init__(self, h, step):
        self.h, self.n, self.step = h, 0, step


class Reg:
    __slots__ = ("t", "c0", "ncols")

    def __init__(self, t, c0, ncols):
        self.t, self.c0, self.ncols = t, c0, ncols

    def ap(self, p0=0, p1=128, c0=0, c1=None):
        if c1 is None:
            c1 = self.ncols
        return self.t.ap()[p0:p1, self.c0 + c0:self.c0 + c1]


class Arena:
    def __init__(self, t, ncols):
        self.t, self.n, self.off = t, ncols, 0

    def take(self, ncols):
        r = Reg(self.t, self.off, ncols)
        self.off += ncols
        assert self.off <= self.n, f"SBUF arena overflow: {self.off} > {self.n}"
        return r

    def reset(self, keep=0):
        self.off = keep


class Stream:

    def __init__(self, kb, name, eng, nsems):
        self.kb, self.eng = kb, eng
        self.sems = [kb._new_sem(f"{name}{i}", 16) for i in range(nsems)]
        self.i = 0

    def batch(self, transfers, waits=()):
        kb = self.kb
        sem = self.sems[self.i % len(self.sems)]
        self.i += 1
        kb.E(self.eng, None, waits=((sem, sem.n),) + tuple(waits))
        for dst, src in transfers:
            kb.E(self.eng, lambda e, d=dst, s=src: e.dma_start(out=d, in_=s),
                 (), inc=sem)
        return (sem, sem.n)


class KB:
    def __init__(self, S_full, S_own, debug=False):
        nc = bass.Bass("TRN2", target_bir_lowering=False)
        self.nc = nc
        self.Sf, self.So = S_full, S_own
        self.NKV = S_full // TOK
        self.NQ = S_own // TOK
        self.q = {e: [] for e in ENGS}
        self.hw = {e: {} for e in ENGS}
        self._sem_ctxs = []

        d = lambda n, s: nc.dram_tensor(n, s, F32, kind="ExternalInput")
        self.xT_full = d("xT_full", [E, S_full])
        self.xT_own = d("xT_own", [E, S_own])
        self.cT = d("cT", [128, FC])
        self.wmod = d("wmod", [E, 6 * E])
        self.w_d_kvT = d("w_d_kvT", [E, C])
        self.w_d_qT = d("w_d_qT", [E, C])
        self.w_u_q = d("w_u_q", [E, C])
        self.w_u_k = d("w_u_k", [E, C])
        self.w_u_v = d("w_u_v", [E, C])
        self.w_oT = d("w_oT", [E, E])
        self.w_qr_pT = d("w_qr_pT", [C, E])
        self.w_kr_pT = d("w_kr_pT", [E, E])
        self.g1c = d("g1c", [128, CCH])
        self.g2c = d("g2c", [128, CCH])
        self.g3c = d("g3c", [128, FC])
        self.gate_wT = d("gate_wT", [E, NEXP])
        self.bias_bc = d("bias_bc", [128, NEXP])
        self.sw1H = d("sw1H", [HCH * 128, E])
        self.sw3H = d("sw3H", [HCH * 128, E])
        self.sw2H = d("sw2H", [FC * 128, HID])
        self.w1H = d("w1H", [NEXP * HCH * 128, E])
        self.w3H = d("w3H", [NEXP * HCH * 128, E])
        self.w2H = d("w2H", [NEXP * FC * 128, HID])
        self.cs_k = d("cs_k", [128, 4 * S_full])
        self.cs_q = d("cs_q", [128, 4 * S_own])
        self.ident = d("ident", [128, 128])
        self.ones = d("ones", [128, 128])
        self.sel8 = d("sel8", [NEXP, NEXP * 128])
        self.epsc = d("epsc", [128, 2])
        self.swp = d("swp", [128, 128])

        self.outT = nc.dram_tensor("outT", [E, S_own], F32, kind="ExternalOutput")
        skind = "ExternalOutput" if debug else "Internal"
        self.Ksc = nc.dram_tensor("Ksc", [C + E, S_full], F32, kind=skind)
        self.Vsc = nc.dram_tensor("Vsc", [S_full, C], F32, kind=skind)
        self.Qsc = nc.dram_tensor("Qsc", [C + E, S_own], F32, kind=skind)
        self.XN = nc.dram_tensor("XN", [E, S_own], F32, kind=skind)

        self.all_sems = []
        for name, step in (("sLD", 16), ("sST", 16), ("sPE", 1), ("sDV", 1),
                           ("sAC", 1)):
            setattr(self, name, self._new_sem(name, step))

        self.P = [nc.alloc_psum_tensor(f"ps{i}", [128, TOK], F32) for i in range(8)]
        self.pfree = [None] * 8

        total = nc.sbuf_bytes_remaining // 4 - 64
        self.sb = nc.alloc_sbuf_tensor("arena", [128, total], F32)
        A = Arena(self.sb, total)
        self.ident_sb = A.take(128)
        self.ones_sb = A.take(128)
        self.sel8_sb = A.take(NEXP * 128)
        self.eps_sb = A.take(2)
        self.swp_sb = A.take(128)
        self.g1_sb = A.take(CCH)
        self.g2_sb = A.take(CCH)
        self.g3_sb = A.take(FC)
        self.cT_sb = A.take(FC)
        self.siluc = A.take(FC)
        self.gatew_sb = A.take(FC * NEXP)
        self.biasbc_sb = A.take(NEXP)
        self.mod_sb = A.take(96)
        self.sa_sb = A.take(FC)
        self.ca1_sb = A.take(FC)
        self.sb_sb = A.take(FC)
        self.cb1_sb = A.take(FC)
        self.gg_sb = A.take(FC)
        self.gbc_sb = A.take(FC)
        self.stats = A.take(2 * TOK)
        self.small = A.take(64)
        self.tmp1 = A.take(TOK)
        self.tmp2 = A.take(TOK)
        self.ropet = None
        self.clamp_t = None
        self.hdr_keep = A.off
        self.wqe = A.take(CCH * C)
        self.awo = A.take(CCH * E)
        self.arena = A
        self.base_keep = A.off

        self._tmp_used = [None, None]
        self._rope_x_used = [None, None]
        self._rope_slot = 0
        self._rope_dv_done = 0
        self._qq_done = None

    def _new_sem(self, name, step):
        ctx = self.nc.semaphore(name)
        s = Sem(ctx.__enter__(), step)
        self._sem_ctxs.append(ctx)
        self.all_sems.append(s)
        return s

    def stream(self, name, eng, nsems=2):
        return Stream(self, name, eng, nsems)

    def E(self, eng, fn, waits=(), inc=None):
        hw = self.hw[eng]
        for t in waits:
            if t is None:
                continue
            s, v = t
            if v > 0 and hw.get(id(s), 0) < v:
                hw[id(s)] = v
                self.q[eng].append(lambda e, h=s.h, vv=v: e.wait_ge(h, vv))
        if fn is None:
            return None
        if inc is None:
            self.q[eng].append(fn)
            return None
        inc.n += inc.step
        v = inc.n
        self.q[eng].append(lambda e, f=fn, h=inc.h, st=inc.step: f(e).then_inc(h, st))
        return v

    def ld(self, dst, src, waits=()):
        return self.E("sync", lambda e, d=dst, s=src: e.dma_start(out=d, in_=s),
                      waits, inc=self.sLD)

    def st(self, dst, src, waits=()):
        return self.E("gpsimd", lambda e, d=dst, s=src: e.dma_start(out=d, in_=s),
                      waits, inc=self.sST)

    def settle(self, engs=("tensor", "vector", "scalar", "sync", "gpsimd")):
        toks = [(s, s.n) for s in self.all_sems]
        for eng in engs:
            self.E(eng, None, waits=toks)

    def puse(self, bank):
        t = self.pfree[bank]
        return (t,) if t else ()

    def mmgrp(self, bank, parts, waits=(), inc_each=False):
        n = len(parts)
        p = self.P[bank].ap()
        toks = []
        for i, (l, r) in enumerate(parts):
            m = l.shape[-1]
            nn = r.shape[-1]
            a, b = (i == 0), (i == n - 1)
            w = (tuple(waits) + self.puse(bank)) if i == 0 else ()
            do_inc = inc_each or b
            v = self.E("tensor",
                       lambda e, o=p[0:m, 0:nn], ll=l, rr=r, a=a, b=b:
                       e.matmul(o, ll, rr, start=a, stop=b),
                       w, inc=(self.sPE if do_inc else None))
            toks.append(v if do_inc else None)
        return self.sPE.n if not inc_each else toks

    def transp(self, bank, in_ap, idn, waits=()):
        w = tuple(waits) + self.puse(bank)
        p = self.P[bank].ap()
        pr = in_ap.shape[-1]
        pc = in_ap.shape[0]
        self.E("tensor", lambda e, o=p[0:pr, 0:pc], i=in_ap, d=idn:
               e.transpose(o, i, d), w, inc=self.sPE)
        return self.sPE.n

    def dv(self, fn, waits=()):
        return self.E("vector", fn, ((self.sDV, self.sDV.n),) + tuple(waits),
                      inc=self.sDV)

    def ac(self, fn, waits=()):
        return self.E("scalar", fn, ((self.sAC, self.sAC.n),) + tuple(waits),
                      inc=self.sAC)

    def sq_tmp(self, slot, src_ap, ntok, extra=()):
        tt = (self.tmp1 if slot == 0 else self.tmp2).ap(c0=0, c1=ntok)
        w = ((self._tmp_used[slot],) if self._tmp_used[slot] else ()) + tuple(extra)
        tok = self.ac(lambda e, o=tt, i=src_ap: e.activation(o, i, AF.Square), w)
        return tt, tok

    def ln_mod(self, hbuf, ntok, scale_sb, shift_sb, ready=()):
        dv, ac = self.dv, self.ac
        for fc in range(FC):
            x = hbuf.ap(c0=fc * TOK, c1=fc * TOK + ntok)
            w0 = (self.puse(0) if fc == 0 else ()) + (tuple(ready) if fc == 0 else ())
            self.E("tensor", lambda e, o=self.P[0].ap()[0:1, 0:ntok],
                   l=self.ones_sb.ap(c0=0, c1=1), r=x, a=(fc == 0), b=(fc == FC - 1):
                   e.matmul(o, l, r, start=a, stop=b), w0,
                   inc=(self.sPE if fc == FC - 1 else None))
            slot = fc % 2
            tt, sq = self.sq_tmp(slot, x, ntok, extra=(tuple(ready) if fc < 2 else ()))
            w1 = self.puse(1) if fc == 0 else ()
            v = self.E("tensor", lambda e, o=self.P[1].ap()[0:1, 0:ntok],
                       l=self.ones_sb.ap(c0=0, c1=1), r=tt, a=(fc == 0),
                       b=(fc == FC - 1): e.matmul(o, l, r, start=a, stop=b),
                       w1 + ((self.sAC, sq),), inc=self.sPE)
            self._tmp_used[slot] = (self.sPE, v)
        sums_done = self.sPE.n
        stp = lambda r: self.stats.ap(p0=0, p1=1, c0=r * TOK, c1=r * TOK + ntok)
        dv(lambda e, o=stp(0), i=self.P[0].ap()[0:1, 0:ntok]:
           e.tensor_scalar_mul(o, i, 1.0 / E), waits=((self.sPE, sums_done),))
        dv(lambda e, o=stp(1), a=stp(0): e.tensor_mul(o, a, a))
        tvar = dv(lambda e, o=stp(1), i=self.P[1].ap()[0:1, 0:ntok], m=stp(1):
                  e.scalar_tensor_tensor(o, i, 1.0 / E, m, ALU.mult, ALU.subtract))
        trs = ac(lambda e, o=stp(1), i=stp(1):
                 e.activation(o, i, AF.Sqrt,
                              bias=self.eps_sb.ap(p0=0, p1=1, c0=0, c1=1)),
                 waits=((self.sDV, tvar),))
        trc = dv(lambda e, o=stp(1), i=stp(1): e.reciprocal(o, i),
                 waits=((self.sAC, trs),))
        tmr = dv(lambda e, o=stp(0), a=stp(0), b=stp(1): e.tensor_mul(o, a, b))
        self.pfree[0] = (self.sDV, tmr)
        self.pfree[1] = (self.sDV, tvar)
        tR = self.mmgrp(2, [(self.ones_sb.ap(p0=0, p1=1), stp(1))],
                        waits=((self.sDV, trc),))
        tMR = self.mmgrp(3, [(self.ones_sb.ap(p0=0, p1=1), stp(0))],
                         waits=((self.sDV, tmr),))
        last_dv = 0
        for fc in range(FC):
            x = hbuf.ap(c0=fc * TOK, c1=fc * TOK + ntok)
            dv(lambda e, o=x, i=x, r=self.P[2].ap()[:, 0:ntok]: e.tensor_mul(o, i, r),
               waits=((self.sPE, tR),))
            last_dv = dv(lambda e, o=x, i=x, r=self.P[3].ap()[:, 0:ntok]:
                         e.tensor_sub(o, i, r), waits=((self.sPE, tMR),))
            last_dv = dv(lambda e, o=x, i=x, s=scale_sb.ap(c0=fc, c1=fc + 1),
                         b=shift_sb.ap(c0=fc, c1=fc + 1):
                         e.tensor_scalar(o, i, s, b, ALU.mult, ALU.add))
        self.pfree[2] = (self.sDV, last_dv)
        self.pfree[3] = (self.sDV, last_dv)
        return last_dv

    def rms_apply(self, src, dst, nch, ntok, g_sb, dim, bank_s, bank_b, ready=()):
        dv, ac = self.dv, self.ac
        for i in range(nch):
            slot = i % 2
            tt, sq = self.sq_tmp(slot, src[i], ntok,
                                 extra=(tuple(ready) if i < 2 else ()))
            w = self.puse(bank_s) if i == 0 else ()
            v = self.E("tensor", lambda e, o=self.P[bank_s].ap()[0:1, 0:ntok],
                       l=self.ones_sb.ap(c0=0, c1=1), r=tt, a=(i == 0),
                       b=(i == nch - 1): e.matmul(o, l, r, start=a, stop=b),
                       w + ((self.sAC, sq),), inc=self.sPE)
            self._tmp_used[slot] = (self.sPE, v)
        tok = self.sPE.n
        rrow = self.stats.ap(p0=0, p1=1, c0=TOK, c1=TOK + ntok)
        trs = ac(lambda e, o=rrow, i=self.P[bank_s].ap()[0:1, 0:ntok]:
                 e.activation(o, i, AF.Sqrt, scale=1.0 / dim,
                              bias=self.eps_sb.ap(p0=0, p1=1, c0=1, c1=2)),
                 waits=((self.sPE, tok),))
        trc = dv(lambda e, o=rrow, i=rrow: e.reciprocal(o, i),
                 waits=((self.sAC, trs),))
        self.pfree[bank_s] = (self.sAC, trs)
        tR = self.mmgrp(bank_b, [(self.ones_sb.ap(p0=0, p1=1), rrow)],
                        waits=((self.sDV, trc),))
        for i in range(nch):
            dv(lambda e, o=dst[i], a=src[i], r=self.P[bank_b].ap()[:, 0:ntok]:
               e.tensor_mul(o, a, r), waits=((self.sPE, tR),))
            self.ac(lambda e, o=dst[i], i_=dst[i], s=g_sb.ap(c0=i, c1=i + 1):
                    e.activation(o, i_, AF.Copy, scale=s),
                    waits=((self.sDV, self.sDV.n),))
        self.pfree[bank_b] = (self.sAC, self.sAC.n)
        return self.sAC.n

    def _rope(self, bank, out_ap, cs_reg, ntok, pe_tok, extra=()):
        dv = self.dv
        blk = lambda i: cs_reg.ap(c0=i * TOK, c1=i * TOK + ntok)
        slot = self._rope_slot
        self._rope_slot ^= 1
        xst = self.ropet.ap(c0=slot * TOK, c1=slot * TOK + ntok)
        tmp = self.ropetmp.ap(c0=0, c1=ntok)
        w = ((self.sPE, pe_tok),) + tuple(extra)
        if self._rope_x_used[slot] is not None:
            w = w + (self._rope_x_used[slot],)
        ev1 = dv(lambda e, o=xst, i=self.P[bank].ap()[:, 0:ntok]:
                 e.tensor_copy(o, i), waits=w)
        self.pfree[bank] = (self.sDV, ev1)
        swap_tok = self.mmgrp(bank, [(self.swp_sb.ap(), xst)])
        rot = self.P[bank].ap()[:, 0:ntok]
        dv(lambda e, o=out_ap, a=xst, b=blk(0): e.tensor_mul(o, a, b),
           waits=tuple(extra))
        dv(lambda e, o=tmp, a=xst, b=blk(1): e.tensor_mul(o, a, b))
        dv(lambda e, o=out_ap, a=out_ap, b=tmp: e.tensor_add(o, a, b))
        dv(lambda e, o=tmp, a=rot, b=blk(2): e.tensor_mul(o, a, b),
           waits=((self.sPE, swap_tok),))
        dv(lambda e, o=out_ap, a=out_ap, b=tmp: e.tensor_add(o, a, b))
        dv(lambda e, o=tmp, a=rot, b=blk(3): e.tensor_mul(o, a, b))
        ev4 = dv(lambda e, o=out_ap, a=out_ap, b=tmp: e.tensor_add(o, a, b))
        self._rope_x_used[slot] = (self.sDV, ev4)
        self.pfree[bank] = (self.sDV, ev4)
        self._rope_dv_done = ev4
        return ev4

    def load16(self, dst_reg, dram, waits=()):
        v = 0
        for g in range(4):
            src = bass.AP(dram, g * 4 * 128 * C, [[C, 128], [128 * C, 4], [1, C]])
            v = self.ld(dst_reg.ap(c0=g * 4 * C, c1=(g + 1) * 4 * C), src,
                        waits=waits)
        return v

    def phase_a(self):
        A = self.arena
        ld, dv, ac = self.ld, self.dv, self.ac
        t0 = 0
        for reg, dram in ((self.ident_sb, self.ident), (self.ones_sb, self.ones),
                          (self.g1_sb, self.g1c), (self.g2_sb, self.g2c),
                          (self.g3_sb, self.g3c), (self.cT_sb, self.cT),
                          (self.biasbc_sb, self.bias_bc)):
            t0 = ld(reg.ap(p0=0, p1=dram.shape[0] if dram.shape[0] < 128 else 128,
                           c0=0, c1=dram.shape[1]), dram.ap())
        gw = bass.AP(self.gate_wT, 0, [[NEXP, 128], [128 * NEXP, FC], [1, NEXP]])
        t0 = ld(self.gatew_sb.ap(), gw)
        t0 = ld(self.sel8_sb.ap(p0=0, p1=NEXP), self.sel8.ap())
        t0 = ld(self.eps_sb.ap(), self.epsc.ap())
        t0 = ld(self.swp_sb.ap(), self.swp.ap())

        wuq = A.take(FC * C)
        wuk = A.take(FC * C)
        wuv = A.take(FC * C)
        wot = A.take(2 * C)
        wmb = A.take(2 * 1024)
        self.clamp_t = A.take(4 * FC)

        self.load16(wuq, self.w_u_q)
        self.load16(wuk, self.w_u_k)
        vv = self.load16(wuv, self.w_u_v)
        vA = self.sLD.n
        vq = vk = vv = vA
        wa = self.stream("wa", "sync", 2)
        wm = self.stream("wm", "sync", 2)
        tsil = ac(lambda e, o=self.siluc.ap(), i=self.cT_sb.ap():
                  e.activation(o, i, AF.Silu), waits=((self.sLD, vA),))

        for mc in range(CCH):
            bank = mc % 2
            parts = [(wuq.ap(c0=kc * C + mc * 128, c1=kc * C + mc * 128 + 128),
                      wuk.ap(c0=kc * C, c1=(kc + 1) * C)) for kc in range(FC)]
            tok = self.mmgrp(bank, parts, waits=((self.sLD, max(vq, vk)),))
            ev = dv(lambda e, o=self.wqe.ap(c0=mc * C, c1=(mc + 1) * C),
                    i=self.P[bank].ap(): e.tensor_copy(o, i),
                    waits=((self.sPE, tok),))
            self.pfree[bank] = (self.sDV, ev)

        wot_used = [0, 0]
        for ncol in range(4):
            for kc in range(FC):
                slot = kc % 2
                wt = wot.ap(c0=slot * C, c1=(slot + 1) * C)
                lw = wa.batch([(wt, self.w_oT.ap()[kc * 128:(kc + 1) * 128,
                                                   ncol * C:(ncol + 1) * C])],
                              waits=((self.sPE, wot_used[slot]),))
                for mc in range(CCH):
                    bank = 4 + mc
                    w = (lw,) + (self.puse(bank) if kc == 0 else ())
                    v = self.E("tensor", lambda e, o=self.P[bank].ap(),
                               l=wuv.ap(c0=kc * C + mc * 128,
                                        c1=kc * C + mc * 128 + 128),
                               r=wt, a=(kc == 0), b=(kc == FC - 1):
                               e.matmul(o, l, r, start=a, stop=b),
                               w, inc=(self.sPE if mc == CCH - 1 else None))
                    if mc == CCH - 1:
                        wot_used[slot] = v
            for mc in range(CCH):
                bank = 4 + mc
                ev = dv(lambda e, o=self.awo.ap(c0=mc * E + ncol * C,
                                                c1=mc * E + (ncol + 1) * C),
                        i=self.P[bank].ap(): e.tensor_copy(o, i),
                        waits=((self.sPE, wot_used[(FC - 1) % 2]),))
                self.pfree[bank] = (self.sDV, ev)

        wm_used = [0, 0]
        for mg in range(12):
            for kc in range(FC):
                slot = kc % 2
                wb = wmb.ap(c0=slot * 1024, c1=(slot + 1) * 1024)
                lw = wm.batch([(wb, self.wmod.ap()[kc * 128:(kc + 1) * 128,
                                                   mg * 1024:(mg + 1) * 1024])],
                              waits=((self.sPE, wm_used[slot]),))
                for j in range(8):
                    wj = (lw, (self.sAC, tsil)) + \
                         (self.puse(j) if kc == 0 else ())
                    v = self.E("tensor",
                               lambda e, o=self.P[j].ap()[:, 0:1],
                               l=wb[:, j * 128:(j + 1) * 128],
                               r=self.siluc.ap(c0=kc, c1=kc + 1),
                               a=(kc == 0), b=(kc == FC - 1):
                               e.matmul(o, l, r, start=a, stop=b),
                               wj, inc=(self.sPE if j == 7 else None))
                    if j == 7:
                        wm_used[slot] = v
            grp_end = self.sPE.n
            for j in range(8):
                m = mg * 8 + j
                ev = dv(lambda e, o=self.mod_sb.ap(c0=m, c1=m + 1),
                        i=self.P[j].ap()[:, 0:1]: e.tensor_copy(o, i),
                        waits=((self.sPE, grp_end),))
                self.pfree[j] = (self.sDV, ev)

        mready = self.sDV.n
        dv(lambda e, o=self.ca1_sb.ap(), i=self.mod_sb.ap(c0=16, c1=32):
           e.tensor_scalar_add(o, i, 1.0), waits=((self.sDV, mready),))
        dv(lambda e, o=self.cb1_sb.ap(), i=self.mod_sb.ap(c0=64, c1=80):
           e.tensor_scalar_add(o, i, 1.0))
        dv(lambda e, o=self.sa_sb.ap(), i=self.mod_sb.ap(c0=0, c1=16):
           e.tensor_copy(o, i))
        dv(lambda e, o=self.sb_sb.ap(), i=self.mod_sb.ap(c0=48, c1=64):
           e.tensor_copy(o, i))

        def clamp(dst_reg, src_ap, extra_mul=None):
            ct = self.clamp_t
            a_ = ct.ap(c0=0, c1=16)
            b_ = ct.ap(c0=16, c1=32)
            m_ = ct.ap(c0=32, c1=48)
            dv(lambda e, o=a_, i=src_ap: e.tensor_scalar_max(o, i, 0.25))
            dv(lambda e, o=b_, i=src_ap: e.tensor_scalar_min(o, i, -0.25))
            dv(lambda e, o=m_, i=src_ap: e.tensor_scalar(o, i, 0.0, None, ALU.is_ge))
            dv(lambda e, o=a_, x=a_, y=b_: e.tensor_sub(o, x, y))
            dv(lambda e, o=a_, x=m_, y=a_: e.tensor_mul(o, x, y))
            dv(lambda e, o=dst_reg.ap(), x=a_, y=b_: e.tensor_add(o, x, y))
            if extra_mul is not None:
                dv(lambda e, o=dst_reg.ap(), i=dst_reg.ap(), g=extra_mul:
                   e.tensor_mul(o, i, g))

        clamp(self.gg_sb, self.mod_sb.ap(c0=32, c1=48), extra_mul=self.g3_sb.ap())
        clamp(self.gbc_sb, self.mod_sb.ap(c0=80, c1=96))

    def phase_b(self):
        A = self.arena
        ld, st, dv, ac = self.ld, self.st, self.dv, self.ac

        A.reset(self.base_keep)
        hbuf = A.take(FC * TOK)
        wdkv = A.take(FC * C)
        ckvp = A.take(CCH * TOK)
        ckv = A.take(CCH * TOK)
        vstg = A.take(CCH * TOK)
        krst = A.take(4 * TOK)
        wkrb = A.take(2 * TOK)
        csbuf = A.take(4 * TOK)
        self.ropet = A.take(2 * TOK)
        self.ropetmp = A.take(TOK)

        self.load16(wdkv, self.w_d_kvT)
        vdkv = self.sLD.n
        cst = self.stream("csk", "sync", 1)
        hb = self.stream("hb", "sync", 1)
        wk = self.stream("wk", "sync", 2)
        sk = self.stream("sk", "gpsimd", 2)
        sv = self.stream("sv", "gpsimd", 2)
        sr = self.stream("sr", "gpsimd", 2)

        h_used_prev = 0
        sk_prev = None
        wkr_used = [0, 0]
        kr_slot_store = [None] * 4
        vstg_store = [None] * 4
        for ci in range(self.NKV):
            cs = ci * TOK
            vx = hb.batch(
                [(hbuf.ap(c0=fc * TOK, c1=(fc + 1) * TOK),
                  self.xT_full.ap()[fc * 128:(fc + 1) * 128, cs:cs + TOK])
                 for fc in range(FC)],
                waits=((self.sPE, h_used_prev),))
            vcs = cst.batch(
                [(csbuf.ap(c0=b4 * TOK, c1=(b4 + 1) * TOK),
                  self.cs_k.ap()[:, b4 * self.Sf + cs:b4 * self.Sf + cs + TOK])
                 for b4 in range(4)],
                waits=((self.sDV, self._rope_dv_done),))
            ready = (vx,)
            h_ready = self.ln_mod(hbuf, TOK, self.ca1_sb, self.sa_sb, ready=ready)

            ckv_ev = 0
            for cc in range(CCH):
                bank = 4 + (cc % 2)
                parts = [(wdkv.ap(c0=kc * C + cc * 128, c1=kc * C + cc * 128 + 128),
                          hbuf.ap(c0=kc * TOK, c1=(kc + 1) * TOK))
                         for kc in range(FC)]
                tok = self.mmgrp(bank, parts,
                                 waits=((self.sLD, vdkv), (self.sDV, h_ready)))
                ckv_ev = dv(lambda e, o=ckvp.ap(c0=cc * TOK, c1=(cc + 1) * TOK),
                            i=self.P[bank].ap(): e.tensor_copy(o, i),
                            waits=((self.sPE, tok),))
                self.pfree[bank] = (self.sDV, ckv_ev)
            src = [ckvp.ap(c0=i * TOK, c1=(i + 1) * TOK) for i in range(CCH)]
            dst = [ckv.ap(c0=i * TOK, c1=(i + 1) * TOK) for i in range(CCH)]
            ckv_done = self.rms_apply(src, dst, CCH, TOK, self.g2_sb, C, 6, 7,
                                      ready=((self.sDV, ckv_ev),) +
                                            ((sk_prev,) if sk_prev else ()))
            sk_prev = sk.batch(
                [(self.Ksc.ap()[cc * 128:(cc + 1) * 128, cs:cs + TOK],
                  ckv.ap(c0=cc * TOK, c1=(cc + 1) * TOK)) for cc in range(CCH)],
                waits=((self.sAC, ckv_done),))
            for ts in range(CCH):
                for cc in range(CCH):
                    bank = 6 + ((ts * CCH + cc) % 2)
                    tok = self.transp(bank,
                                      ckv.ap(c0=cc * TOK + ts * 128,
                                             c1=cc * TOK + ts * 128 + 128),
                                      self.ident_sb.ap(),
                                      waits=((self.sAC, ckv_done),))
                    dv(lambda e, o=vstg.ap(c0=ts * C + cc * 128,
                                           c1=ts * C + (cc + 1) * 128),
                       i=self.P[bank].ap()[:, 0:128]: e.tensor_copy(o, i),
                       waits=((self.sPE, tok),) +
                             ((vstg_store[ts],) if vstg_store[ts] else ()))
                    self.pfree[bank] = (self.sDV, self.sDV.n)
                vstg_store[ts] = sv.batch(
                    [(self.Vsc.ap()[cs + ts * 128:cs + (ts + 1) * 128, 0:C],
                      vstg.ap(c0=ts * C, c1=(ts + 1) * C))],
                    waits=((self.sDV, self.sDV.n),))
            for mg in range(4):
                for kc in range(FC):
                    slot = kc % 2
                    wb = wkrb.ap(c0=slot * TOK, c1=(slot + 1) * TOK)
                    lw = wk.batch(
                        [(wb, self.w_kr_pT.ap()[kc * 128:(kc + 1) * 128,
                                                mg * 512:(mg + 1) * 512])],
                        waits=((self.sPE, wkr_used[slot]),))
                    for j in range(4):
                        bank = 4 + j
                        wj = (lw,) + (self.puse(bank) if kc == 0 else ())
                        v = self.E("tensor",
                                   lambda e, o=self.P[bank].ap(),
                                   l=wb[:, j * 128:(j + 1) * 128],
                                   r=hbuf.ap(c0=kc * TOK, c1=(kc + 1) * TOK),
                                   a=(kc == 0), b=(kc == FC - 1):
                                   e.matmul(o, l, r, start=a, stop=b),
                                   wj, inc=(self.sPE if j == 3 else None))
                        if j == 3:
                            wkr_used[slot] = v
                grp_end = self.sPE.n
                for j in range(4):
                    bank = 4 + j
                    hd = mg * 4 + j
                    ks = krst.ap(c0=j * TOK, c1=(j + 1) * TOK)
                    ev = self._rope(bank, ks, csbuf, TOK, grp_end,
                                    extra=((kr_slot_store[j],)
                                           if kr_slot_store[j] else ()) + (vcs,))
                    kr_slot_store[j] = sr.batch(
                        [(self.Ksc.ap()[C + hd * 128:C + (hd + 1) * 128,
                                        cs:cs + TOK], ks)],
                        waits=((self.sDV, ev),))
            h_used_prev = self.sPE.n

        self.settle()
        A.reset(self.base_keep)
        hbuf = A.take(FC * TOK)
        wdq = A.take(FC * C)
        wqr = A.take(CCH * E)
        tT = A.take(CCH * TOK)
        cq = A.take(CCH * TOK)
        qst = A.take(4 * TOK)
        csbuf = A.take(4 * TOK)
        self.ropet = A.take(2 * TOK)
        self.ropetmp = A.take(TOK)

        self.load16(wdq, self.w_d_qT)
        for g in range(4):
            src = bass.AP(self.w_qr_pT, g * 128 * E, [[E, 128], [1, E]])
            ld(wqr.ap(c0=g * E, c1=(g + 1) * E), src)
        vdq = vqr = self.sLD.n
        cst = self.stream("csq", "sync", 1)
        hb = self.stream("hq", "sync", 1)
        sq = self.stream("sq", "gpsimd", 2)

        h_used_prev = 0
        q_slot_store = [None] * 4
        for ci in range(self.NQ):
            cs = ci * TOK
            vx = hb.batch(
                [(hbuf.ap(c0=fc * TOK, c1=(fc + 1) * TOK),
                  self.xT_own.ap()[fc * 128:(fc + 1) * 128, cs:cs + TOK])
                 for fc in range(FC)],
                waits=((self.sPE, h_used_prev),))
            vcs = cst.batch(
                [(csbuf.ap(c0=b4 * TOK, c1=(b4 + 1) * TOK),
                  self.cs_q.ap()[:, b4 * self.So + cs:b4 * self.So + cs + TOK])
                 for b4 in range(4)],
                waits=((self.sDV, self._rope_dv_done),))
            ready = (vx,)
            h_ready = self.ln_mod(hbuf, TOK, self.ca1_sb, self.sa_sb, ready=ready)
            t_ev = 0
            for cc in range(CCH):
                bank = 4 + (cc % 2)
                parts = [(wdq.ap(c0=kc * C + cc * 128, c1=kc * C + cc * 128 + 128),
                          hbuf.ap(c0=kc * TOK, c1=(kc + 1) * TOK))
                         for kc in range(FC)]
                tok = self.mmgrp(bank, parts,
                                 waits=((self.sLD, vdq), (self.sDV, h_ready)))
                t_ev = dv(lambda e, o=tT.ap(c0=cc * TOK, c1=(cc + 1) * TOK),
                          i=self.P[bank].ap(): e.tensor_copy(o, i),
                          waits=((self.sPE, tok),))
                self.pfree[bank] = (self.sDV, t_ev)
            for mc in range(CCH):
                bank = 4 + (mc % 2)
                parts = [(self.wqe.ap(c0=kc * C + mc * 128,
                                      c1=kc * C + mc * 128 + 128),
                          tT.ap(c0=kc * TOK, c1=(kc + 1) * TOK))
                         for kc in range(CCH)]
                tok = self.mmgrp(bank, parts, waits=((self.sDV, t_ev),))
                slot = mc % 4
                ev = dv(lambda e, o=qst.ap(c0=slot * TOK, c1=(slot + 1) * TOK),
                        i=self.P[bank].ap(): e.tensor_copy(o, i),
                        waits=((self.sPE, tok),) +
                              ((q_slot_store[slot],) if q_slot_store[slot] else ()))
                self.pfree[bank] = (self.sDV, ev)
                q_slot_store[slot] = sq.batch(
                    [(self.Qsc.ap()[mc * 128:(mc + 1) * 128, cs:cs + TOK],
                      qst.ap(c0=slot * TOK, c1=(slot + 1) * TOK))],
                    waits=((self.sDV, ev),))
            src = [tT.ap(c0=i * TOK, c1=(i + 1) * TOK) for i in range(CCH)]
            dst = [cq.ap(c0=i * TOK, c1=(i + 1) * TOK) for i in range(CCH)]
            cq_done = self.rms_apply(src, dst, CCH, TOK, self.g1_sb, C, 6, 7,
                                     ready=((self.sDV, t_ev),))
            for hd in range(FC):
                bank = 4 + (hd % 2)
                parts = [(wqr.ap(c0=kc * E + hd * 128, c1=kc * E + hd * 128 + 128),
                          cq.ap(c0=kc * TOK, c1=(kc + 1) * TOK))
                         for kc in range(CCH)]
                tok = self.mmgrp(bank, parts,
                                 waits=((self.sLD, vqr), (self.sAC, cq_done)))
                slot = hd % 4
                ks = qst.ap(c0=slot * TOK, c1=(slot + 1) * TOK)
                ev = self._rope(bank, ks, csbuf, TOK, tok,
                                extra=((q_slot_store[slot],)
                                       if q_slot_store[slot] else ()) + (vcs,))
                q_slot_store[slot] = sq.batch(
                    [(self.Qsc.ap()[C + hd * 128:C + (hd + 1) * 128, cs:cs + TOK],
                      ks)], waits=((self.sDV, ev),))
            h_used_prev = self.sPE.n

    def phase_c(self):
        A = self.arena
        ld, st, dv, ac = self.ld, self.st, self.dv, self.ac
        self.settle()
        A.reset(self.base_keep)
        qbuf = A.take(QC * TOK)
        kbuf = A.take(2 * QC * TOK)
        vbuf = A.take(CCH * TOK)
        abuf = A.take(4 * TOK)
        atb = A.take(4 * TOK)
        obuf = A.take(4 * TOK)
        ubuf = Reg(self.sb, kbuf.c0, FC * TOK)
        xbuf = Reg(self.sb, kbuf.c0 + FC * TOK, FC * TOK)
        otb = Reg(self.sb, kbuf.c0 + 2 * FC * TOK, CCH * TOK)
        assert 2 * FC * TOK + CCH * TOK <= 2 * QC * TOK

        sm = self.small
        m_run = sm.ap(c0=0, c1=4)
        l_run = sm.ap(c0=4, c1=8)
        fcol = sm.ap(c0=8, c1=12)
        mold = sm.ap(c0=12, c1=16)
        mx = sm.ap(c0=16, c1=20)
        ps_ = sm.ap(c0=20, c1=24)
        negm = sm.ap(c0=24, c1=25)

        NKB = self.Sf // TOK
        qst_ = self.stream("cq", "sync", 1)
        kst = self.stream("ck", "sync", 2)
        vst = self.stream("cv", "sync", 2)
        xst = self.stream("cx", "sync", 1)
        sx = self.stream("sx", "gpsimd", 1)
        qq_done = None
        for qq in range(self.NQ):
            qs = qq * TOK
            wq0 = (qq_done or ())
            vq = qst_.batch(
                [(qbuf.ap(c0=t * TOK, c1=(t + 1) * TOK),
                  self.Qsc.ap()[t * 128:(t + 1) * 128, qs:qs + TOK])
                 for t in range(QC)], waits=wq0)
            k_used = [0, 0]
            v_used = 0
            for kb in range(NKB):
                ks = kb * TOK
                slot = kb % 2
                kslot = Reg(self.sb, kbuf.c0 + slot * QC * TOK, QC * TOK)
                w = ((self.sPE, k_used[slot]),) if kb >= 2 else wq0
                vk = kst.batch(
                    [(kslot.ap(c0=t * TOK, c1=(t + 1) * TOK),
                      self.Ksc.ap()[t * 128:(t + 1) * 128, ks:ks + TOK])
                     for t in range(QC)], waits=w)
                vsrc = bass.AP(self.Vsc, ks * C, [[C, 128], [128 * C, 4], [1, C]])
                wv = ((self.sPE, v_used),) if kb >= 1 else wq0
                vv = vst.batch([(vbuf.ap(), vsrc)], waits=wv)
                a_ready = []
                for qb in range(4):
                    bank = qb % 2
                    parts = [(qbuf.ap(c0=t * TOK + qb * 128,
                                      c1=t * TOK + qb * 128 + 128),
                              kslot.ap(c0=t * TOK, c1=(t + 1) * TOK))
                             for t in range(QC)]
                    tok = self.mmgrp(bank, parts, waits=(vq, vk))
                    if qb == 3:
                        k_used[slot] = self.sPE.n
                    q1 = lambda apx, qb=qb: apx[:, qb:qb + 1]
                    dv(lambda e, o=q1(mx), i=self.P[bank].ap():
                       e.reduce_max(o, i, axis=AX.X), waits=((self.sPE, tok),))
                    if kb == 0:
                        dv(lambda e, o=q1(m_run), i=q1(mx): e.tensor_copy(o, i))
                    else:
                        dv(lambda e, o=q1(mold), i=q1(m_run): e.tensor_copy(o, i))
                        dv(lambda e, o=q1(m_run), a=q1(m_run), b=q1(mx):
                           e.tensor_max(o, a, b))
                        tdf = dv(lambda e, o=q1(mold), a=q1(mold), b=q1(m_run):
                                 e.tensor_sub(o, a, b))
                        ac(lambda e, o=q1(fcol), i=q1(mold):
                           e.activation(o, i, AF.Exp, scale=SMX),
                           waits=((self.sDV, tdf),))
                    tneg = dv(lambda e, o=negm, i=q1(m_run):
                              e.tensor_scalar_mul(o, i, -SMX))
                    texp = ac(lambda e, o=abuf.ap(c0=qb * TOK, c1=(qb + 1) * TOK),
                              i=self.P[bank].ap(), b=negm:
                              e.activation(o, i, AF.Exp, bias=b, scale=SMX),
                              waits=((self.sDV, tneg),))
                    self.pfree[bank] = (self.sAC, texp)
                    dv(lambda e, o=q1(ps_),
                       i=abuf.ap(c0=qb * TOK, c1=(qb + 1) * TOK):
                       e.reduce_sum(o, i, axis=AX.X), waits=((self.sAC, texp),))
                    if kb == 0:
                        dv(lambda e, o=q1(l_run), i=q1(ps_): e.tensor_copy(o, i))
                    else:
                        dv(lambda e, o=q1(l_run), i=q1(l_run), s=q1(fcol),
                           p=q1(ps_):
                           e.scalar_tensor_tensor(o, i, s, p, ALU.mult, ALU.add),
                           waits=((self.sAC, self.sAC.n),))
                    for kc in range(4):
                        tb = 2 + (kc % 2)
                        tok2 = self.transp(tb,
                                           abuf.ap(c0=qb * TOK + kc * 128,
                                                   c1=qb * TOK + kc * 128 + 128),
                                           self.ident_sb.ap(),
                                           waits=((self.sAC, texp),))
                        ev = dv(lambda e,
                                o=atb.ap(c0=kc * TOK + qb * 128,
                                         c1=kc * TOK + (qb + 1) * 128),
                                i=self.P[tb].ap()[:, 0:128]: e.tensor_copy(o, i),
                                waits=((self.sPE, tok2),))
                        self.pfree[tb] = (self.sDV, ev)
                    a_ready.append(self.sDV.n)
                for qb in range(4):
                    bank = 4 + (qb % 2)
                    parts = [(atb.ap(c0=kc * TOK + qb * 128,
                                     c1=kc * TOK + (qb + 1) * 128),
                              vbuf.ap(c0=kc * TOK, c1=(kc + 1) * TOK))
                             for kc in range(4)]
                    tok = self.mmgrp(bank, parts,
                                     waits=((self.sDV, a_ready[qb]), vv))
                    if qb == 3:
                        v_used = self.sPE.n
                    ob = obuf.ap(c0=qb * TOK, c1=(qb + 1) * TOK)
                    if kb == 0:
                        ev = dv(lambda e, o=ob, i=self.P[bank].ap():
                                e.tensor_copy(o, i), waits=((self.sPE, tok),))
                    else:
                        ev = dv(lambda e, o=ob, i=ob, s=fcol[:, qb:qb + 1],
                                p=self.P[bank].ap():
                                e.scalar_tensor_tensor(o, i, s, p, ALU.mult,
                                                       ALU.add),
                                waits=((self.sPE, tok),))
                    self.pfree[bank] = (self.sDV, ev)
            scores_pe_done = self.sPE.n
            rl = sm.ap(c0=28, c1=32)
            dv(lambda e, o=rl, i=l_run: e.reciprocal(o, i))
            for qb in range(4):
                ob = obuf.ap(c0=qb * TOK, c1=(qb + 1) * TOK)
                dv(lambda e, o=ob, i=ob, s=rl[:, qb:qb + 1]:
                   e.tensor_scalar(o, i, s, None, ALU.mult))
            o_done = self.sDV.n
            for qb in range(4):
                for cc in range(CCH):
                    tb = 2 + ((qb * CCH + cc) % 2)
                    tok = self.transp(tb, obuf.ap(c0=qb * TOK + cc * 128,
                                                  c1=qb * TOK + cc * 128 + 128),
                                      self.ident_sb.ap(),
                                      waits=((self.sDV, o_done),))
                    ev = dv(lambda e, o=otb.ap(c0=cc * TOK + qb * 128,
                                               c1=cc * TOK + (qb + 1) * 128),
                            i=self.P[tb].ap()[:, 0:128]: e.tensor_copy(o, i),
                            waits=((self.sPE, tok),))
                    self.pfree[tb] = (self.sDV, ev)
            ot_done = self.sDV.n
            u_ev = 0
            for mc in range(FC):
                bank = 4 + (mc % 2)
                parts = [(self.awo.ap(c0=cc * E + mc * 128,
                                      c1=cc * E + mc * 128 + 128),
                          otb.ap(c0=cc * TOK, c1=(cc + 1) * TOK))
                         for cc in range(CCH)]
                tok = self.mmgrp(bank, parts, waits=((self.sDV, ot_done),))
                u_ev = dv(lambda e, o=ubuf.ap(c0=mc * TOK, c1=(mc + 1) * TOK),
                          i=self.P[bank].ap(): e.tensor_copy(o, i),
                          waits=((self.sPE, tok),))
                self.pfree[bank] = (self.sDV, u_ev)
            for i in range(FC):
                slot = i % 2
                tt, sq = self.sq_tmp(slot, ubuf.ap(c0=i * TOK, c1=(i + 1) * TOK),
                                     TOK, extra=((self.sDV, u_ev),) if i < 2 else ())
                w = self.puse(0) if i == 0 else ()
                v = self.E("tensor", lambda e, o=self.P[0].ap()[0:1, :],
                           l=self.ones_sb.ap(c0=0, c1=1), r=tt, a=(i == 0),
                           b=(i == FC - 1): e.matmul(o, l, r, start=a, stop=b),
                           w + ((self.sAC, sq),), inc=self.sPE)
                self._tmp_used[slot] = (self.sPE, v)
            rrow = self.stats.ap(p0=0, p1=1, c0=TOK, c1=2 * TOK)
            trs = ac(lambda e, o=rrow, i=self.P[0].ap()[0:1, :]:
                     e.activation(o, i, AF.Sqrt, scale=1.0 / E,
                                  bias=self.eps_sb.ap(p0=0, p1=1, c0=1, c1=2)),
                     waits=((self.sPE, self.sPE.n),))
            trc = dv(lambda e, o=rrow, i=rrow: e.reciprocal(o, i),
                     waits=((self.sAC, trs),))
            self.pfree[0] = (self.sAC, trs)
            tR = self.mmgrp(1, [(self.ones_sb.ap(p0=0, p1=1), rrow)],
                            waits=((self.sDV, trc),))
            vx = xst.batch(
                [(xbuf.ap(c0=fc * TOK, c1=(fc + 1) * TOK),
                  self.xT_own.ap()[fc * 128:(fc + 1) * 128, qs:qs + TOK])
                 for fc in range(FC)],
                waits=((self.sPE, scores_pe_done),))
            last = 0
            for fc in range(FC):
                u = ubuf.ap(c0=fc * TOK, c1=(fc + 1) * TOK)
                x = xbuf.ap(c0=fc * TOK, c1=(fc + 1) * TOK)
                dv(lambda e, o=u, i=u, r=self.P[1].ap(): e.tensor_mul(o, i, r),
                   waits=((self.sPE, tR), vx))
                last = dv(lambda e, o=x, i=u, s=self.gg_sb.ap(c0=fc, c1=fc + 1),
                          xx=x: e.scalar_tensor_tensor(o, i, s, xx, ALU.mult,
                                                       ALU.add))
            sxt = sx.batch(
                [(self.XN.ap()[fc * 128:(fc + 1) * 128, qs:qs + TOK],
                  xbuf.ap(c0=fc * TOK, c1=(fc + 1) * TOK)) for fc in range(FC)],
                waits=((self.sDV, last),))
            self.pfree[1] = (self.sDV, last)
            qq_done = ((self.sPE, self.sPE.n), (self.sDV, self.sDV.n),
                       (self.sAC, self.sAC.n), sxt)

    def phase_d(self):
        A = self.arena
        ld, st, dv, ac = self.ld, self.st, self.dv, self.ac
        self.settle()
        A.reset(self.hdr_keep)
        xn = A.take(FC * TOK)
        h2 = A.take(FC * TOK)
        xs = A.take(FC * TOK)
        racc = A.take(FC * TOK)
        zbuf = A.take(HCH * TOK)
        w13 = A.take(2 * 2 * FC * 128)
        w2b = A.take(2 * HCH * 128)
        scg = A.take(TOK)
        wsb = A.take(TOK)
        vt = A.take(4 * NEXP)
        sct = A.take(4 * NEXP)
        wt_ = A.take(4 * NEXP)
        g2_ = A.take(96)

        xnst = self.stream("dx", "sync", 1)
        w13st = self.stream("w13", "sync", 2)
        w2st = self.stream("w2", "sync", 2)
        so = self.stream("so", "gpsimd", 1)
        w13_used = [0, 0]
        w2_used = [0, 0]
        xn_prev = None
        for ci in range(self.NQ):
            cs = ci * TOK
            vx = xnst.batch(
                [(xn.ap(c0=fc * TOK, c1=(fc + 1) * TOK),
                  self.XN.ap()[fc * 128:(fc + 1) * 128, cs:cs + TOK])
                 for fc in range(FC)],
                waits=((xn_prev,) if xn_prev else ()))
            cp_done = 0
            for fc in range(FC):
                cp_done = dv(lambda e, o=h2.ap(c0=fc * TOK, c1=(fc + 1) * TOK),
                             i=xn.ap(c0=fc * TOK, c1=(fc + 1) * TOK):
                             e.tensor_copy(o, i), waits=(vx,))
            h2_ready = self.ln_mod(h2, TOK, self.cb1_sb, self.sb_sb,
                                   ready=((self.sDV, cp_done),))

            parts = [(self.gatew_sb.ap(c0=kc * NEXP, c1=(kc + 1) * NEXP),
                      h2.ap(c0=kc * TOK, c1=(kc + 1) * TOK)) for kc in range(FC)]
            tok = self.mmgrp(0, parts, waits=((self.sDV, h2_ready),))
            tsg = ac(lambda e, o=scg.ap(p0=0, p1=NEXP),
                     i=self.P[0].ap()[0:NEXP, :]:
                     e.activation(o, i, AF.Sigmoid), waits=((self.sPE, tok),))
            self.pfree[0] = (self.sAC, tsg)
            for ts in range(4):
                tb = 2 + (ts % 2)
                tok2 = self.transp(tb, scg.ap(p0=0, p1=NEXP, c0=ts * 128,
                                              c1=(ts + 1) * 128),
                                   self.ident_sb.ap(p0=0, p1=NEXP, c0=0, c1=NEXP),
                                   waits=((self.sAC, tsg),))
                sc_t = sct.ap(c0=ts * NEXP, c1=(ts + 1) * NEXP)
                v_t = vt.ap(c0=ts * NEXP, c1=(ts + 1) * NEXP)
                dv(lambda e, o=sc_t, i=self.P[tb].ap()[:, 0:NEXP]:
                   e.tensor_copy(o, i), waits=((self.sPE, tok2),))
                ev = dv(lambda e, o=v_t, a=self.P[tb].ap()[:, 0:NEXP],
                        b=self.biasbc_sb.ap(): e.tensor_add(o, a, b))
                self.pfree[tb] = (self.sDV, ev)
                self._top2(v_t, sc_t, wt_.ap(c0=ts * NEXP, c1=(ts + 1) * NEXP), g2_)
            for ts in range(4):
                tb = 2 + (ts % 2)
                tok2 = self.transp(tb, wt_.ap(c0=ts * NEXP, c1=(ts + 1) * NEXP),
                                   self.ident_sb.ap(),
                                   waits=((self.sDV, self.sDV.n),))
                ev = dv(lambda e, o=wsb.ap(p0=0, p1=NEXP, c0=ts * 128,
                                           c1=(ts + 1) * 128),
                        i=self.P[tb].ap()[0:NEXP, 0:128]: e.tensor_copy(o, i),
                        waits=((self.sPE, tok2),))
                self.pfree[tb] = (self.sDV, ev)
            w_ready = self.sDV.n

            for ei in range(NEXP + 1):
                e_ = ei - 1
                if ei == 0:
                    xsrc = h2
                    xs_tok = (self.sDV, h2_ready)
                else:
                    bb = 6 + (ei % 2)
                    tok = self.mmgrp(bb, [(self.sel8_sb.ap(p0=0, p1=NEXP,
                                                             c0=e_ * 128,
                                                             c1=(e_ + 1) * 128),
                                           wsb.ap(p0=0, p1=NEXP))],
                                     waits=((self.sDV, w_ready),))
                    lastx = 0
                    for fc in range(FC):
                        lastx = dv(lambda e, o=xs.ap(c0=fc * TOK, c1=(fc + 1) * TOK),
                                   a=h2.ap(c0=fc * TOK, c1=(fc + 1) * TOK),
                                   b=self.P[bb].ap(): e.tensor_mul(o, a, b),
                                   waits=((self.sPE, tok),))
                    self.pfree[bb] = (self.sDV, lastx)
                    xsrc = xs
                    xs_tok = (self.sDV, lastx)
                z_ev = 0
                for hc in range(HCH):
                    slot = hc % 2
                    wreg = Reg(self.sb, w13.c0 + slot * 2 * FC * 128, 2 * FC * 128)
                    if ei == 0:
                        o1, o3 = hc * 128 * E, hc * 128 * E
                        d1, d3 = self.sw1H, self.sw3H
                    else:
                        o1 = o3 = (e_ * HCH + hc) * 128 * E
                        d1, d3 = self.w1H, self.w3H
                    s1 = bass.AP(d1, o1, [[E, 128], [1, E]])
                    s3 = bass.AP(d3, o3, [[E, 128], [1, E]])
                    l3 = w13st.batch(
                        [(wreg.ap(c0=0, c1=FC * 128), s1),
                         (wreg.ap(c0=FC * 128, c1=2 * FC * 128), s3)],
                        waits=((self.sPE, w13_used[slot]),))
                    pb1, pb3 = (0, 1) if slot == 0 else (2, 3)
                    parts1 = [(wreg.ap(c0=kc * 128, c1=(kc + 1) * 128),
                               xsrc.ap(c0=kc * TOK, c1=(kc + 1) * TOK))
                              for kc in range(FC)]
                    parts3 = [(wreg.ap(c0=FC * 128 + kc * 128,
                                       c1=FC * 128 + (kc + 1) * 128),
                               xsrc.ap(c0=kc * TOK, c1=(kc + 1) * TOK))
                              for kc in range(FC)]
                    self.mmgrp(pb1, parts1, waits=(l3, xs_tok))
                    t3 = self.mmgrp(pb3, parts3)
                    w13_used[slot] = t3
                    tt, tz1 = None, None
                    ttap = (self.tmp1 if slot == 0 else self.tmp2).ap()
                    wz = ((self._tmp_used[slot],) if self._tmp_used[slot] else ())
                    tz1 = ac(lambda e, o=ttap, i=self.P[pb1].ap():
                             e.activation(o, i, AF.Silu),
                             waits=((self.sPE, t3),) + wz)
                    z_ev = dv(lambda e, o=zbuf.ap(c0=hc * TOK, c1=(hc + 1) * TOK),
                              a=ttap, b=self.P[pb3].ap(): e.tensor_mul(o, a, b),
                              waits=((self.sAC, tz1),))
                    self._tmp_used[slot] = (self.sDV, z_ev)
                    self.pfree[pb1] = (self.sAC, tz1)
                    self.pfree[pb3] = (self.sDV, z_ev)
                for oc in range(FC):
                    slot = oc % 2
                    w2reg = Reg(self.sb, w2b.c0 + slot * HCH * 128, HCH * 128)
                    if ei == 0:
                        s2 = bass.AP(self.sw2H, oc * 128 * HID, [[HID, 128], [1, HID]])
                    else:
                        s2 = bass.AP(self.w2H, (e_ * FC + oc) * 128 * HID,
                                     [[HID, 128], [1, HID]])
                    l2 = w2st.batch([(w2reg.ap(), s2)],
                                    waits=((self.sPE, w2_used[slot]),))
                    bank = 4 + slot
                    parts = [(w2reg.ap(c0=kc * 128, c1=(kc + 1) * 128),
                              zbuf.ap(c0=kc * TOK, c1=(kc + 1) * TOK))
                             for kc in range(HCH)]
                    tok = self.mmgrp(bank, parts,
                                     waits=(l2, (self.sDV, z_ev)))
                    w2_used[slot] = tok
                    r = racc.ap(c0=oc * TOK, c1=(oc + 1) * TOK)
                    if ei == 0:
                        ev = dv(lambda e, o=r, i=self.P[bank].ap():
                                e.tensor_copy(o, i), waits=((self.sPE, tok),))
                    else:
                        ev = dv(lambda e, o=r, a=r, b=self.P[bank].ap():
                                e.tensor_add(o, a, b), waits=((self.sPE, tok),))
                    self.pfree[bank] = (self.sDV, ev)
            last = 0
            for fc in range(FC):
                r = racc.ap(c0=fc * TOK, c1=(fc + 1) * TOK)
                x = xn.ap(c0=fc * TOK, c1=(fc + 1) * TOK)
                last = dv(lambda e, o=x, i=r, s=self.gbc_sb.ap(c0=fc, c1=fc + 1),
                          xx=x: e.scalar_tensor_tensor(o, i, s, xx, ALU.mult,
                                                       ALU.add))
            xn_prev = so.batch(
                [(self.outT.ap()[fc * 128:(fc + 1) * 128, cs:cs + TOK],
                  xn.ap(c0=fc * TOK, c1=(fc + 1) * TOK)) for fc in range(FC)],
                waits=((self.sDV, last),))

    def _top2(self, v_ap, sc_ap, w_ap, g2_):
        dv = self.dv
        a = lambda c0, c1: g2_.ap(c0=c0, c1=c1)
        vmax = a(0, 1)
        eq = a(8, 16)
        c2 = a(16, 24)
        c3 = a(24, 32)
        c4 = a(32, 40)
        iz = a(40, 48)
        m1 = a(48, 56)
        m2 = a(56, 64)
        v2 = a(64, 72)

        def first_max(vin, mout):
            dv(lambda e, o=vmax, i=vin: e.reduce_max(o, i, axis=AX.X))
            dv(lambda e, o=eq, i=vin, s=vmax:
               e.tensor_scalar(o, i, s, None, ALU.is_ge))
            dv(lambda e, o=c2[:, 0:1], i=eq[:, 0:1]: e.tensor_copy(o, i))
            dv(lambda e, o=c2[:, 1:8], x=eq[:, 1:8], y=eq[:, 0:7]:
               e.tensor_add(o, x, y))
            dv(lambda e, o=c3[:, 0:2], i=c2[:, 0:2]: e.tensor_copy(o, i))
            dv(lambda e, o=c3[:, 2:8], x=c2[:, 2:8], y=c2[:, 0:6]:
               e.tensor_add(o, x, y))
            dv(lambda e, o=c4[:, 0:4], i=c3[:, 0:4]: e.tensor_copy(o, i))
            dv(lambda e, o=c4[:, 4:8], x=c3[:, 4:8], y=c3[:, 0:4]:
               e.tensor_add(o, x, y))
            dv(lambda e, o=c4, x=c4, y=eq: e.tensor_sub(o, x, y))
            dv(lambda e, o=iz, i=c4: e.tensor_scalar(o, i, 0.5, None, ALU.is_le))
            dv(lambda e, o=mout, x=eq, y=iz: e.tensor_mul(o, x, y))

        first_max(v_ap, m1)
        dv(lambda e, o=v2, i=m1, x=v_ap:
           e.scalar_tensor_tensor(o, i, -1e5, x, ALU.mult, ALU.add))
        first_max(v2, m2)
        dv(lambda e, o=m1, x=m1, y=m2: e.tensor_add(o, x, y))
        dv(lambda e, o=w_ap, x=m1, y=sc_ap: e.tensor_mul(o, x, y))

    def build(self):
        self.phase_a()
        self.settle()
        self.phase_b()
        self.phase_c()
        self.phase_d()
        self.settle()

        nc = self.nc
        with nc.Block() as blk:
            for eng, attr in (("sync", blk.sync), ("gpsimd", blk.gpsimd),
                              ("tensor", blk.tensor), ("vector", blk.vector),
                              ("scalar", blk.scalar)):
                def run(e, eng=eng):
                    for f in self.q[eng]:
                        f(e)
                attr(run)
        for ctx in self._sem_ctxs:
            ctx.__exit__(None, None, None)
        nc.finalize()
        return nc



def _rope_tables(S):
    D = 128
    cos = sin = None
    try:
        import jax
        import jax.numpy as jnp
        cpu = jax.devices("cpu")[0]
        with jax.default_device(cpu):
            freqs = 1.0 / 10000.0 ** (jnp.arange(0, D, 2, dtype=jnp.float32) / D)
            ang = jnp.arange(S, dtype=jnp.float32)[:, None] * freqs[None, :]
            cos = np.asarray(jax.device_get(jnp.cos(ang))).T
            sin = np.asarray(jax.device_get(jnp.sin(ang))).T
    except Exception:
        pass
    if cos is None:
        e32 = (np.arange(0, D, 2).astype(np.float32) / np.float32(D))
        freqs = np.float32(1.0) / (np.float64(10000.0)
                                   ** e32.astype(np.float64)).astype(np.float32)
        ang = (np.arange(S, dtype=np.float32)[:, None]
               * freqs[None, :]).astype(np.float32)
        cos = np.cos(ang.astype(np.float64)).astype(np.float32).T
        sin = np.sin(ang.astype(np.float64)).astype(np.float32).T
    A = np.concatenate([cos, cos], axis=0).astype(np.float32)
    B = np.concatenate([-sin, sin], axis=0).astype(np.float32)
    Z = np.zeros_like(A)
    return np.concatenate([A, Z, B, Z], axis=1)


def _head_perm():
    p = np.empty(E, dtype=np.int64)
    for h in range(16):
        base = h * 128
        p[base:base + 64] = base + np.arange(0, 128, 2)
        p[base + 64:base + 128] = base + np.arange(1, 128, 2)
    return p


def _tile_w13(w):
    n = w.shape[0]
    wt = w.transpose(0, 2, 1).reshape(n, FC, 128, HCH, 128)
    return np.ascontiguousarray(wt.transpose(0, 3, 2, 1, 4)).reshape(n * HCH * 128, E)


def _tile_w2(w):
    n = w.shape[0]
    wt = w.transpose(0, 2, 1).reshape(n, HCH, 128, FC, 128)
    return np.ascontiguousarray(wt.transpose(0, 3, 2, 1, 4)).reshape(n * FC * 128, HID)


def prep_inputs(inputs, S_full, S_own, cores):
    cc = np.ascontiguousarray
    perm = _head_perm()
    x = np.asarray(inputs["L0_attn_h"], dtype=np.float32)
    cs_k = cc(_rope_tables(S_full))
    swp = np.zeros((128, 128), dtype=np.float32)
    for i in range(128):
        swp[i, i ^ 64] = 1.0

    shared = {
        "wmod": cc(np.asarray(inputs["L0_wmod"], np.float32)),
        "w_d_kvT": cc(np.asarray(inputs["L0_attn_w_d_kv"], np.float32).T),
        "w_d_qT": cc(np.asarray(inputs["L0_attn_w_d_q"], np.float32).T),
        "w_u_q": cc(np.asarray(inputs["L0_attn_w_u_q"], np.float32)),
        "w_u_k": cc(np.asarray(inputs["L0_attn_w_u_k"], np.float32)),
        "w_u_v": cc(np.asarray(inputs["L0_attn_w_u_v"], np.float32)),
        "w_oT": cc(np.asarray(inputs["L0_attn_w_o"], np.float32).T),
        "w_qr_pT": cc(np.asarray(inputs["L0_attn_w_qr"], np.float32)[perm].T),
        "w_kr_pT": cc(np.asarray(inputs["L0_attn_w_kr"], np.float32)[perm].T),
        "g1c": cc(np.asarray(inputs["L0_attn_g1"], np.float32).reshape(CCH, 128).T),
        "g2c": cc(np.asarray(inputs["L0_attn_g2"], np.float32).reshape(CCH, 128).T),
        "g3c": cc(np.asarray(inputs["L0_attn_g3"], np.float32).reshape(FC, 128).T),
        "gate_wT": cc(np.asarray(inputs["L0_ffn_gate_w"], np.float32).T),
        "bias_bc": cc(np.broadcast_to(
            np.asarray(inputs["L0_ffn_expert_bias"], np.float32)[None, :],
            (128, NEXP)).copy()),
        "sw1H": _tile_w13(np.asarray(inputs["L0_ffn_sw1"], np.float32)[None]),
        "sw3H": _tile_w13(np.asarray(inputs["L0_ffn_sw3"], np.float32)[None]),
        "sw2H": _tile_w2(np.asarray(inputs["L0_ffn_sw2"], np.float32)[None]),
        "w1H": _tile_w13(np.asarray(inputs["L0_ffn_w1"], np.float32)),
        "w3H": _tile_w13(np.asarray(inputs["L0_ffn_w3"], np.float32)),
        "w2H": _tile_w2(np.asarray(inputs["L0_ffn_w2"], np.float32)),
        "cs_k": cs_k,
        "ident": np.eye(128, dtype=np.float32),
        "ones": np.ones((128, 128), dtype=np.float32),
        "epsc": np.broadcast_to(np.array([EPS_LN, EPS_RMS], np.float32)[None, :],
                                (128, 2)).copy(),
        "swp": swp,
        "sel8": np.concatenate(
            [np.where(np.arange(NEXP)[:, None] == e_,
                      np.float32(1.0), np.float32(0.0)).repeat(128, axis=1)
             for e_ in range(NEXP)], axis=1).astype(np.float32),
    }
    cmat = np.asarray(inputs["L0_c"], np.float32)
    in_maps = []
    for core in range(cores):
        b = (core // 2) % x.shape[0]
        h = core % 2
        xT = cc(x[b].T)
        own = slice(h * S_own, (h + 1) * S_own)
        m = dict(shared)
        m["xT_full"] = xT
        m["xT_own"] = cc(xT[:, own])
        m["cT"] = cc(cmat[b].reshape(FC, 128).T)
        m["cs_q"] = cc(np.concatenate(
            [cs_k[:, b4 * S_full + h * S_own:b4 * S_full + (h + 1) * S_own]
             for b4 in range(4)], axis=1))
        in_maps.append(m)
    return in_maps


def kernel(**inputs):
    x = np.asarray(inputs["L0_attn_h"])
    B, S_full, _ = x.shape
    n_cores = 8
    S_own = S_full * B // n_cores
    kb = KB(S_full, S_own)
    nc = kb.build()
    in_maps = prep_inputs(inputs, S_full, S_own, n_cores)
    res = run_bass_kernel_spmd(nc, in_maps, core_ids=list(range(n_cores)))
    out = np.empty((B, S_full, E), dtype=np.float32)
    for core in range(n_cores):
        b = core // 2
        h = core % 2
        out[b, h * S_own:(h + 1) * S_own, :] = \
            np.asarray(res.results[core]["outT"]).T
    return out


# version 06
# speedup vs baseline: 1.0256x; 1.0256x over previous
import numpy as np
import concourse.bass as bass
import concourse.mybir as mybir
from concourse.bass_utils import run_bass_kernel_spmd

F32 = mybir.dt.float32
AF = mybir.ActivationFunctionType
ALU = mybir.AluOpType
AX = mybir.AxisListType

EPS_RMS = float(np.finfo(np.float64).eps)
EPS_LN = 1e-5
SMX_HI = float(np.float32(1.0 / 12.0))
SMX_LO = float(np.float32(1.0 / 12.0 - np.float64(np.float32(1.0 / 12.0))))
E = 2048
C = 512
HID = 1024
NEXP = 8
FC = E // 128
CCH = C // 128
HCH = HID // 128
TOK = 512
QC = (C + E) // 128
ENGS = ("sync", "gpsimd", "tensor", "vector", "scalar")


class Sem:
    __slots__ = ("h", "n", "step")

    def __init__(self, h, step):
        self.h, self.n, self.step = h, 0, step


class Reg:
    __slots__ = ("t", "c0", "ncols")

    def __init__(self, t, c0, ncols):
        self.t, self.c0, self.ncols = t, c0, ncols

    def ap(self, p0=0, p1=128, c0=0, c1=None):
        if c1 is None:
            c1 = self.ncols
        return self.t.ap()[p0:p1, self.c0 + c0:self.c0 + c1]


class Arena:
    def __init__(self, t, ncols):
        self.t, self.n, self.off = t, ncols, 0

    def take(self, ncols):
        r = Reg(self.t, self.off, ncols)
        self.off += ncols
        assert self.off <= self.n, f"SBUF arena overflow: {self.off} > {self.n}"
        return r

    def reset(self, keep=0):
        self.off = keep


class Stream:

    def __init__(self, kb, name, eng, nsems):
        self.kb, self.eng = kb, eng
        self.sems = [kb._new_sem(f"{name}{i}", 16) for i in range(nsems)]
        self.i = 0

    def batch(self, transfers, waits=()):
        kb = self.kb
        sem = self.sems[self.i % len(self.sems)]
        self.i += 1
        kb.E(self.eng, None, waits=((sem, sem.n),) + tuple(waits))
        for dst, src in transfers:
            kb.E(self.eng, lambda e, d=dst, s=src: e.dma_start(out=d, in_=s),
                 (), inc=sem)
        return (sem, sem.n)


class KB:
    def __init__(self, S_full, S_own, debug=False):
        nc = bass.Bass("TRN2", target_bir_lowering=False)
        self.nc = nc
        self.Sf, self.So = S_full, S_own
        self.NKV = S_full // TOK
        self.NQ = S_own // TOK
        self.q = {e: [] for e in ENGS}
        self.hw = {e: {} for e in ENGS}
        self._sem_ctxs = []

        d = lambda n, s: nc.dram_tensor(n, s, F32, kind="ExternalInput")
        self.xT_full = d("xT_full", [E, S_full])
        self.xT_own = d("xT_own", [E, S_own])
        self.cT = d("cT", [128, FC])
        self.wmod = d("wmod", [E, 6 * E])
        self.w_d_kvT = d("w_d_kvT", [E, C])
        self.w_d_qT = d("w_d_qT", [E, C])
        self.w_u_q = d("w_u_q", [E, C])
        self.w_u_k = d("w_u_k", [E, C])
        self.w_u_v = d("w_u_v", [E, C])
        self.w_oT = d("w_oT", [E, E])
        self.w_qr_pT = d("w_qr_pT", [C, E])
        self.w_kr_pT = d("w_kr_pT", [E, E])
        self.g1c = d("g1c", [128, CCH])
        self.g2c = d("g2c", [128, CCH])
        self.g3c = d("g3c", [128, FC])
        self.gate_wT = d("gate_wT", [E, NEXP])
        self.bias_bc = d("bias_bc", [128, NEXP])
        self.sw1H = d("sw1H", [HCH * 128, E])
        self.sw3H = d("sw3H", [HCH * 128, E])
        self.sw2H = d("sw2H", [FC * 128, HID])
        self.w1H = d("w1H", [NEXP * HCH * 128, E])
        self.w3H = d("w3H", [NEXP * HCH * 128, E])
        self.w2H = d("w2H", [NEXP * FC * 128, HID])
        self.cs_k = d("cs_k", [128, 4 * S_full])
        self.cs_q = d("cs_q", [128, 4 * S_own])
        self.ident = d("ident", [128, 128])
        self.ones = d("ones", [128, 128])
        self.sel8 = d("sel8", [NEXP, NEXP * 128])
        self.epsc = d("epsc", [128, 2])
        self.swp = d("swp", [128, 128])

        self.outT = nc.dram_tensor("outT", [E, S_own], F32, kind="ExternalOutput")
        skind = "ExternalOutput" if debug else "Internal"
        self.Ksc = nc.dram_tensor("Ksc", [C + E, S_full], F32, kind=skind)
        self.Vsc = nc.dram_tensor("Vsc", [S_full, C], F32, kind=skind)
        self.Qsc = nc.dram_tensor("Qsc", [C + E, S_own], F32, kind=skind)
        self.XN = nc.dram_tensor("XN", [E, S_own], F32, kind=skind)

        self.all_sems = []
        for name, step in (("sLD", 16), ("sST", 16), ("sPE", 1), ("sDV", 1),
                           ("sAC", 1)):
            setattr(self, name, self._new_sem(name, step))

        self.P = [nc.alloc_psum_tensor(f"ps{i}", [128, TOK], F32) for i in range(8)]
        self.pfree = [None] * 8

        total = nc.sbuf_bytes_remaining // 4 - 64
        self.sb = nc.alloc_sbuf_tensor("arena", [128, total], F32)
        A = Arena(self.sb, total)
        self.ident_sb = A.take(128)
        self.ones_sb = A.take(128)
        self.sel8_sb = A.take(NEXP * 128)
        self.eps_sb = A.take(2)
        self.swp_sb = A.take(128)
        self.g1_sb = A.take(CCH)
        self.g2_sb = A.take(CCH)
        self.g3_sb = A.take(FC)
        self.cT_sb = A.take(FC)
        self.siluc = A.take(FC)
        self.gatew_sb = A.take(FC * NEXP)
        self.biasbc_sb = A.take(NEXP)
        self.mod_sb = A.take(96)
        self.sa_sb = A.take(FC)
        self.ca1_sb = A.take(FC)
        self.sb_sb = A.take(FC)
        self.cb1_sb = A.take(FC)
        self.gg_sb = A.take(FC)
        self.gbc_sb = A.take(FC)
        self.stats = A.take(2 * TOK)
        self.small = A.take(64)
        self.tmp1 = A.take(TOK)
        self.tmp2 = A.take(TOK)
        self.ropet = None
        self.clamp_t = None
        self.hdr_keep = A.off
        self.wqe = A.take(CCH * C)
        self.awo = A.take(CCH * E)
        self.arena = A
        self.base_keep = A.off

        self._tmp_used = [None, None]
        self._rope_x_used = [None, None]
        self._rope_slot = 0
        self._rope_dv_done = 0
        self._qq_done = None

    def _new_sem(self, name, step):
        ctx = self.nc.semaphore(name)
        s = Sem(ctx.__enter__(), step)
        self._sem_ctxs.append(ctx)
        self.all_sems.append(s)
        return s

    def stream(self, name, eng, nsems=2):
        return Stream(self, name, eng, nsems)

    def E(self, eng, fn, waits=(), inc=None):
        hw = self.hw[eng]
        for t in waits:
            if t is None:
                continue
            s, v = t
            if v > 0 and hw.get(id(s), 0) < v:
                hw[id(s)] = v
                self.q[eng].append(lambda e, h=s.h, vv=v: e.wait_ge(h, vv))
        if fn is None:
            return None
        if inc is None:
            self.q[eng].append(fn)
            return None
        inc.n += inc.step
        v = inc.n
        self.q[eng].append(lambda e, f=fn, h=inc.h, st=inc.step: f(e).then_inc(h, st))
        return v

    def ld(self, dst, src, waits=()):
        return self.E("sync", lambda e, d=dst, s=src: e.dma_start(out=d, in_=s),
                      waits, inc=self.sLD)

    def st(self, dst, src, waits=()):
        return self.E("gpsimd", lambda e, d=dst, s=src: e.dma_start(out=d, in_=s),
                      waits, inc=self.sST)

    def settle(self, engs=("tensor", "vector", "scalar", "sync", "gpsimd")):
        toks = [(s, s.n) for s in self.all_sems]
        for eng in engs:
            self.E(eng, None, waits=toks)

    def puse(self, bank):
        t = self.pfree[bank]
        return (t,) if t else ()

    def mmgrp(self, bank, parts, waits=(), inc_each=False):
        n = len(parts)
        p = self.P[bank].ap()
        toks = []
        for i, (l, r) in enumerate(parts):
            m = l.shape[-1]
            nn = r.shape[-1]
            a, b = (i == 0), (i == n - 1)
            w = (tuple(waits) + self.puse(bank)) if i == 0 else ()
            do_inc = inc_each or b
            v = self.E("tensor",
                       lambda e, o=p[0:m, 0:nn], ll=l, rr=r, a=a, b=b:
                       e.matmul(o, ll, rr, start=a, stop=b),
                       w, inc=(self.sPE if do_inc else None))
            toks.append(v if do_inc else None)
        return self.sPE.n if not inc_each else toks

    def transp(self, bank, in_ap, idn, waits=()):
        w = tuple(waits) + self.puse(bank)
        p = self.P[bank].ap()
        pr = in_ap.shape[-1]
        pc = in_ap.shape[0]
        self.E("tensor", lambda e, o=p[0:pr, 0:pc], i=in_ap, d=idn:
               e.transpose(o, i, d), w, inc=self.sPE)
        return self.sPE.n

    def dv(self, fn, waits=()):
        return self.E("vector", fn, ((self.sDV, self.sDV.n),) + tuple(waits),
                      inc=self.sDV)

    def ac(self, fn, waits=()):
        return self.E("scalar", fn, ((self.sAC, self.sAC.n),) + tuple(waits),
                      inc=self.sAC)

    def sq_tmp(self, slot, src_ap, ntok, extra=()):
        tt = (self.tmp1 if slot == 0 else self.tmp2).ap(c0=0, c1=ntok)
        w = ((self._tmp_used[slot],) if self._tmp_used[slot] else ()) + tuple(extra)
        tok = self.ac(lambda e, o=tt, i=src_ap: e.activation(o, i, AF.Square), w)
        return tt, tok

    def ln_mod(self, hbuf, ntok, scale_sb, shift_sb, ready=()):
        dv, ac = self.dv, self.ac
        for fc in range(FC):
            x = hbuf.ap(c0=fc * TOK, c1=fc * TOK + ntok)
            w0 = (self.puse(0) if fc == 0 else ()) + (tuple(ready) if fc == 0 else ())
            self.E("tensor", lambda e, o=self.P[0].ap()[0:1, 0:ntok],
                   l=self.ones_sb.ap(c0=0, c1=1), r=x, a=(fc == 0), b=(fc == FC - 1):
                   e.matmul(o, l, r, start=a, stop=b), w0,
                   inc=(self.sPE if fc == FC - 1 else None))
            slot = fc % 2
            tt, sq = self.sq_tmp(slot, x, ntok, extra=(tuple(ready) if fc < 2 else ()))
            w1 = self.puse(1) if fc == 0 else ()
            v = self.E("tensor", lambda e, o=self.P[1].ap()[0:1, 0:ntok],
                       l=self.ones_sb.ap(c0=0, c1=1), r=tt, a=(fc == 0),
                       b=(fc == FC - 1): e.matmul(o, l, r, start=a, stop=b),
                       w1 + ((self.sAC, sq),), inc=self.sPE)
            self._tmp_used[slot] = (self.sPE, v)
        sums_done = self.sPE.n
        stp = lambda r: self.stats.ap(p0=0, p1=1, c0=r * TOK, c1=r * TOK + ntok)
        dv(lambda e, o=stp(0), i=self.P[0].ap()[0:1, 0:ntok]:
           e.tensor_scalar_mul(o, i, 1.0 / E), waits=((self.sPE, sums_done),))
        dv(lambda e, o=stp(1), a=stp(0): e.tensor_mul(o, a, a))
        tvar = dv(lambda e, o=stp(1), i=self.P[1].ap()[0:1, 0:ntok], m=stp(1):
                  e.scalar_tensor_tensor(o, i, 1.0 / E, m, ALU.mult, ALU.subtract))
        trs = ac(lambda e, o=stp(1), i=stp(1):
                 e.activation(o, i, AF.Sqrt,
                              bias=self.eps_sb.ap(p0=0, p1=1, c0=0, c1=1)),
                 waits=((self.sDV, tvar),))
        trc = dv(lambda e, o=stp(1), i=stp(1): e.reciprocal(o, i),
                 waits=((self.sAC, trs),))
        tmr = dv(lambda e, o=stp(0), a=stp(0), b=stp(1): e.tensor_mul(o, a, b))
        self.pfree[0] = (self.sDV, tmr)
        self.pfree[1] = (self.sDV, tvar)
        tR = self.mmgrp(2, [(self.ones_sb.ap(p0=0, p1=1), stp(1))],
                        waits=((self.sDV, trc),))
        tMR = self.mmgrp(3, [(self.ones_sb.ap(p0=0, p1=1), stp(0))],
                         waits=((self.sDV, tmr),))
        last_dv = 0
        for fc in range(FC):
            x = hbuf.ap(c0=fc * TOK, c1=fc * TOK + ntok)
            dv(lambda e, o=x, i=x, r=self.P[2].ap()[:, 0:ntok]: e.tensor_mul(o, i, r),
               waits=((self.sPE, tR),))
            last_dv = dv(lambda e, o=x, i=x, r=self.P[3].ap()[:, 0:ntok]:
                         e.tensor_sub(o, i, r), waits=((self.sPE, tMR),))
            last_dv = dv(lambda e, o=x, i=x, s=scale_sb.ap(c0=fc, c1=fc + 1),
                         b=shift_sb.ap(c0=fc, c1=fc + 1):
                         e.tensor_scalar(o, i, s, b, ALU.mult, ALU.add))
        self.pfree[2] = (self.sDV, last_dv)
        self.pfree[3] = (self.sDV, last_dv)
        return last_dv

    def rms_apply(self, src, dst, nch, ntok, g_sb, dim, bank_s, bank_b, ready=()):
        dv, ac = self.dv, self.ac
        for i in range(nch):
            slot = i % 2
            tt, sq = self.sq_tmp(slot, src[i], ntok,
                                 extra=(tuple(ready) if i < 2 else ()))
            w = self.puse(bank_s) if i == 0 else ()
            v = self.E("tensor", lambda e, o=self.P[bank_s].ap()[0:1, 0:ntok],
                       l=self.ones_sb.ap(c0=0, c1=1), r=tt, a=(i == 0),
                       b=(i == nch - 1): e.matmul(o, l, r, start=a, stop=b),
                       w + ((self.sAC, sq),), inc=self.sPE)
            self._tmp_used[slot] = (self.sPE, v)
        tok = self.sPE.n
        rrow = self.stats.ap(p0=0, p1=1, c0=TOK, c1=TOK + ntok)
        trs = ac(lambda e, o=rrow, i=self.P[bank_s].ap()[0:1, 0:ntok]:
                 e.activation(o, i, AF.Sqrt, scale=1.0 / dim,
                              bias=self.eps_sb.ap(p0=0, p1=1, c0=1, c1=2)),
                 waits=((self.sPE, tok),))
        trc = dv(lambda e, o=rrow, i=rrow: e.reciprocal(o, i),
                 waits=((self.sAC, trs),))
        self.pfree[bank_s] = (self.sAC, trs)
        tR = self.mmgrp(bank_b, [(self.ones_sb.ap(p0=0, p1=1), rrow)],
                        waits=((self.sDV, trc),))
        for i in range(nch):
            dv(lambda e, o=dst[i], a=src[i], r=self.P[bank_b].ap()[:, 0:ntok]:
               e.tensor_mul(o, a, r), waits=((self.sPE, tR),))
            self.ac(lambda e, o=dst[i], i_=dst[i], s=g_sb.ap(c0=i, c1=i + 1):
                    e.activation(o, i_, AF.Copy, scale=s),
                    waits=((self.sDV, self.sDV.n),))
        self.pfree[bank_b] = (self.sAC, self.sAC.n)
        return self.sAC.n

    def _rope(self, bank, out_ap, cs_reg, ntok, pe_tok, extra=()):
        dv = self.dv
        blk = lambda i: cs_reg.ap(c0=i * TOK, c1=i * TOK + ntok)
        slot = self._rope_slot
        self._rope_slot ^= 1
        xst = self.ropet.ap(c0=slot * TOK, c1=slot * TOK + ntok)
        tmp = self.ropetmp.ap(c0=0, c1=ntok)
        w = ((self.sPE, pe_tok),) + tuple(extra)
        if self._rope_x_used[slot] is not None:
            w = w + (self._rope_x_used[slot],)
        ev1 = dv(lambda e, o=xst, i=self.P[bank].ap()[:, 0:ntok]:
                 e.tensor_copy(o, i), waits=w)
        self.pfree[bank] = (self.sDV, ev1)
        swap_tok = self.mmgrp(bank, [(self.swp_sb.ap(), xst)])
        rot = self.P[bank].ap()[:, 0:ntok]
        dv(lambda e, o=out_ap, a=xst, b=blk(0): e.tensor_mul(o, a, b),
           waits=tuple(extra))
        dv(lambda e, o=tmp, a=xst, b=blk(1): e.tensor_mul(o, a, b))
        dv(lambda e, o=out_ap, a=out_ap, b=tmp: e.tensor_add(o, a, b))
        dv(lambda e, o=tmp, a=rot, b=blk(2): e.tensor_mul(o, a, b),
           waits=((self.sPE, swap_tok),))
        dv(lambda e, o=out_ap, a=out_ap, b=tmp: e.tensor_add(o, a, b))
        dv(lambda e, o=tmp, a=rot, b=blk(3): e.tensor_mul(o, a, b))
        ev4 = dv(lambda e, o=out_ap, a=out_ap, b=tmp: e.tensor_add(o, a, b))
        self._rope_x_used[slot] = (self.sDV, ev4)
        self.pfree[bank] = (self.sDV, ev4)
        self._rope_dv_done = ev4
        return ev4

    def load16(self, dst_reg, dram, waits=()):
        v = 0
        for g in range(4):
            src = bass.AP(dram, g * 4 * 128 * C, [[C, 128], [128 * C, 4], [1, C]])
            v = self.ld(dst_reg.ap(c0=g * 4 * C, c1=(g + 1) * 4 * C), src,
                        waits=waits)
        return v

    def phase_a(self):
        A = self.arena
        ld, dv, ac = self.ld, self.dv, self.ac
        t0 = 0
        for reg, dram in ((self.ident_sb, self.ident), (self.ones_sb, self.ones),
                          (self.g1_sb, self.g1c), (self.g2_sb, self.g2c),
                          (self.g3_sb, self.g3c), (self.cT_sb, self.cT),
                          (self.biasbc_sb, self.bias_bc)):
            t0 = ld(reg.ap(p0=0, p1=dram.shape[0] if dram.shape[0] < 128 else 128,
                           c0=0, c1=dram.shape[1]), dram.ap())
        gw = bass.AP(self.gate_wT, 0, [[NEXP, 128], [128 * NEXP, FC], [1, NEXP]])
        t0 = ld(self.gatew_sb.ap(), gw)
        t0 = ld(self.sel8_sb.ap(p0=0, p1=NEXP), self.sel8.ap())
        t0 = ld(self.eps_sb.ap(), self.epsc.ap())
        t0 = ld(self.swp_sb.ap(), self.swp.ap())

        wuq = A.take(FC * C)
        wuk = A.take(FC * C)
        wuv = A.take(FC * C)
        wot = A.take(2 * C)
        wmb = A.take(2 * 1024)
        self.clamp_t = A.take(4 * FC)

        self.load16(wuq, self.w_u_q)
        self.load16(wuk, self.w_u_k)
        vv = self.load16(wuv, self.w_u_v)
        vA = self.sLD.n
        vq = vk = vv = vA
        wa = self.stream("wa", "sync", 2)
        wm = self.stream("wm", "sync", 2)
        tsil = ac(lambda e, o=self.siluc.ap(), i=self.cT_sb.ap():
                  e.activation(o, i, AF.Silu), waits=((self.sLD, vA),))

        for mc in range(CCH):
            bank = mc % 2
            parts = [(wuq.ap(c0=kc * C + mc * 128, c1=kc * C + mc * 128 + 128),
                      wuk.ap(c0=kc * C, c1=(kc + 1) * C)) for kc in range(FC)]
            tok = self.mmgrp(bank, parts, waits=((self.sLD, max(vq, vk)),))
            dv(lambda e, o=self.wqe.ap(c0=mc * C, c1=(mc + 1) * C),
               i=self.P[bank].ap(): e.tensor_scalar(o, i, SMX_HI, None, ALU.mult),
               waits=((self.sPE, tok),))
            ev = dv(lambda e, o=self.wqe.ap(c0=mc * C, c1=(mc + 1) * C),
                    i=self.P[bank].ap(), oo=self.wqe.ap(c0=mc * C, c1=(mc + 1) * C):
                    e.scalar_tensor_tensor(oo, i, SMX_LO, oo, ALU.mult, ALU.add),
                    waits=((self.sPE, tok),))
            self.pfree[bank] = (self.sDV, ev)

        wot_used = [0, 0]
        for ncol in range(4):
            for kc in range(FC):
                slot = kc % 2
                wt = wot.ap(c0=slot * C, c1=(slot + 1) * C)
                lw = wa.batch([(wt, self.w_oT.ap()[kc * 128:(kc + 1) * 128,
                                                   ncol * C:(ncol + 1) * C])],
                              waits=((self.sPE, wot_used[slot]),))
                for mc in range(CCH):
                    bank = 4 + mc
                    w = (lw,) + (self.puse(bank) if kc == 0 else ())
                    v = self.E("tensor", lambda e, o=self.P[bank].ap(),
                               l=wuv.ap(c0=kc * C + mc * 128,
                                        c1=kc * C + mc * 128 + 128),
                               r=wt, a=(kc == 0), b=(kc == FC - 1):
                               e.matmul(o, l, r, start=a, stop=b),
                               w, inc=(self.sPE if mc == CCH - 1 else None))
                    if mc == CCH - 1:
                        wot_used[slot] = v
            for mc in range(CCH):
                bank = 4 + mc
                ev = dv(lambda e, o=self.awo.ap(c0=mc * E + ncol * C,
                                                c1=mc * E + (ncol + 1) * C),
                        i=self.P[bank].ap(): e.tensor_copy(o, i),
                        waits=((self.sPE, wot_used[(FC - 1) % 2]),))
                self.pfree[bank] = (self.sDV, ev)

        wm_used = [0, 0]
        for mg in range(12):
            for kc in range(FC):
                slot = kc % 2
                wb = wmb.ap(c0=slot * 1024, c1=(slot + 1) * 1024)
                lw = wm.batch([(wb, self.wmod.ap()[kc * 128:(kc + 1) * 128,
                                                   mg * 1024:(mg + 1) * 1024])],
                              waits=((self.sPE, wm_used[slot]),))
                for j in range(8):
                    wj = (lw, (self.sAC, tsil)) + \
                         (self.puse(j) if kc == 0 else ())
                    v = self.E("tensor",
                               lambda e, o=self.P[j].ap()[:, 0:1],
                               l=wb[:, j * 128:(j + 1) * 128],
                               r=self.siluc.ap(c0=kc, c1=kc + 1),
                               a=(kc == 0), b=(kc == FC - 1):
                               e.matmul(o, l, r, start=a, stop=b),
                               wj, inc=(self.sPE if j == 7 else None))
                    if j == 7:
                        wm_used[slot] = v
            grp_end = self.sPE.n
            for j in range(8):
                m = mg * 8 + j
                ev = dv(lambda e, o=self.mod_sb.ap(c0=m, c1=m + 1),
                        i=self.P[j].ap()[:, 0:1]: e.tensor_copy(o, i),
                        waits=((self.sPE, grp_end),))
                self.pfree[j] = (self.sDV, ev)

        mready = self.sDV.n
        dv(lambda e, o=self.ca1_sb.ap(), i=self.mod_sb.ap(c0=16, c1=32):
           e.tensor_scalar_add(o, i, 1.0), waits=((self.sDV, mready),))
        dv(lambda e, o=self.cb1_sb.ap(), i=self.mod_sb.ap(c0=64, c1=80):
           e.tensor_scalar_add(o, i, 1.0))
        dv(lambda e, o=self.sa_sb.ap(), i=self.mod_sb.ap(c0=0, c1=16):
           e.tensor_copy(o, i))
        dv(lambda e, o=self.sb_sb.ap(), i=self.mod_sb.ap(c0=48, c1=64):
           e.tensor_copy(o, i))

        def clamp(dst_reg, src_ap, extra_mul=None):
            ct = self.clamp_t
            a_ = ct.ap(c0=0, c1=16)
            b_ = ct.ap(c0=16, c1=32)
            m_ = ct.ap(c0=32, c1=48)
            dv(lambda e, o=a_, i=src_ap: e.tensor_scalar_max(o, i, 0.25))
            dv(lambda e, o=b_, i=src_ap: e.tensor_scalar_min(o, i, -0.25))
            dv(lambda e, o=m_, i=src_ap: e.tensor_scalar(o, i, 0.0, None, ALU.is_ge))
            dv(lambda e, o=a_, x=a_, y=b_: e.tensor_sub(o, x, y))
            dv(lambda e, o=a_, x=m_, y=a_: e.tensor_mul(o, x, y))
            dv(lambda e, o=dst_reg.ap(), x=a_, y=b_: e.tensor_add(o, x, y))
            if extra_mul is not None:
                dv(lambda e, o=dst_reg.ap(), i=dst_reg.ap(), g=extra_mul:
                   e.tensor_mul(o, i, g))

        clamp(self.gg_sb, self.mod_sb.ap(c0=32, c1=48), extra_mul=self.g3_sb.ap())
        clamp(self.gbc_sb, self.mod_sb.ap(c0=80, c1=96))

    def phase_b(self):
        A = self.arena
        ld, st, dv, ac = self.ld, self.st, self.dv, self.ac

        A.reset(self.base_keep)
        hbuf = A.take(FC * TOK)
        wdkv = A.take(FC * C)
        ckvp = A.take(CCH * TOK)
        ckv = A.take(CCH * TOK)
        vstg = A.take(CCH * TOK)
        krst = A.take(4 * TOK)
        wkrb = A.take(2 * TOK)
        csbuf = A.take(4 * TOK)
        self.ropet = A.take(2 * TOK)
        self.ropetmp = A.take(TOK)

        self.load16(wdkv, self.w_d_kvT)
        vdkv = self.sLD.n
        cst = self.stream("csk", "sync", 1)
        hb = self.stream("hb", "sync", 1)
        wk = self.stream("wk", "sync", 2)
        sk = self.stream("sk", "gpsimd", 2)
        sv = self.stream("sv", "gpsimd", 2)
        sr = self.stream("sr", "gpsimd", 2)

        h_used_prev = 0
        sk_prev = None
        wkr_used = [0, 0]
        kr_slot_store = [None] * 4
        vstg_store = [None] * 4
        for ci in range(self.NKV):
            cs = ci * TOK
            vx = hb.batch(
                [(hbuf.ap(c0=fc * TOK, c1=(fc + 1) * TOK),
                  self.xT_full.ap()[fc * 128:(fc + 1) * 128, cs:cs + TOK])
                 for fc in range(FC)],
                waits=((self.sPE, h_used_prev),))
            vcs = cst.batch(
                [(csbuf.ap(c0=b4 * TOK, c1=(b4 + 1) * TOK),
                  self.cs_k.ap()[:, b4 * self.Sf + cs:b4 * self.Sf + cs + TOK])
                 for b4 in range(4)],
                waits=((self.sDV, self._rope_dv_done),))
            ready = (vx,)
            h_ready = self.ln_mod(hbuf, TOK, self.ca1_sb, self.sa_sb, ready=ready)

            ckv_ev = 0
            for cc in range(CCH):
                bank = 4 + (cc % 2)
                parts = [(wdkv.ap(c0=kc * C + cc * 128, c1=kc * C + cc * 128 + 128),
                          hbuf.ap(c0=kc * TOK, c1=(kc + 1) * TOK))
                         for kc in range(FC)]
                tok = self.mmgrp(bank, parts,
                                 waits=((self.sLD, vdkv), (self.sDV, h_ready)))
                ckv_ev = dv(lambda e, o=ckvp.ap(c0=cc * TOK, c1=(cc + 1) * TOK),
                            i=self.P[bank].ap(): e.tensor_copy(o, i),
                            waits=((self.sPE, tok),))
                self.pfree[bank] = (self.sDV, ckv_ev)
            src = [ckvp.ap(c0=i * TOK, c1=(i + 1) * TOK) for i in range(CCH)]
            dst = [ckv.ap(c0=i * TOK, c1=(i + 1) * TOK) for i in range(CCH)]
            ckv_done = self.rms_apply(src, dst, CCH, TOK, self.g2_sb, C, 6, 7,
                                      ready=((self.sDV, ckv_ev),) +
                                            ((sk_prev,) if sk_prev else ()))
            sk_prev = sk.batch(
                [(self.Ksc.ap()[cc * 128:(cc + 1) * 128, cs:cs + TOK],
                  ckv.ap(c0=cc * TOK, c1=(cc + 1) * TOK)) for cc in range(CCH)],
                waits=((self.sAC, ckv_done),))
            for ts in range(CCH):
                for cc in range(CCH):
                    bank = 6 + ((ts * CCH + cc) % 2)
                    tok = self.transp(bank,
                                      ckv.ap(c0=cc * TOK + ts * 128,
                                             c1=cc * TOK + ts * 128 + 128),
                                      self.ident_sb.ap(),
                                      waits=((self.sAC, ckv_done),))
                    dv(lambda e, o=vstg.ap(c0=ts * C + cc * 128,
                                           c1=ts * C + (cc + 1) * 128),
                       i=self.P[bank].ap()[:, 0:128]: e.tensor_copy(o, i),
                       waits=((self.sPE, tok),) +
                             ((vstg_store[ts],) if vstg_store[ts] else ()))
                    self.pfree[bank] = (self.sDV, self.sDV.n)
                vstg_store[ts] = sv.batch(
                    [(self.Vsc.ap()[cs + ts * 128:cs + (ts + 1) * 128, 0:C],
                      vstg.ap(c0=ts * C, c1=(ts + 1) * C))],
                    waits=((self.sDV, self.sDV.n),))
            for mg in range(4):
                for kc in range(FC):
                    slot = kc % 2
                    wb = wkrb.ap(c0=slot * TOK, c1=(slot + 1) * TOK)
                    lw = wk.batch(
                        [(wb, self.w_kr_pT.ap()[kc * 128:(kc + 1) * 128,
                                                mg * 512:(mg + 1) * 512])],
                        waits=((self.sPE, wkr_used[slot]),))
                    for j in range(4):
                        bank = 4 + j
                        wj = (lw,) + (self.puse(bank) if kc == 0 else ())
                        v = self.E("tensor",
                                   lambda e, o=self.P[bank].ap(),
                                   l=wb[:, j * 128:(j + 1) * 128],
                                   r=hbuf.ap(c0=kc * TOK, c1=(kc + 1) * TOK),
                                   a=(kc == 0), b=(kc == FC - 1):
                                   e.matmul(o, l, r, start=a, stop=b),
                                   wj, inc=(self.sPE if j == 3 else None))
                        if j == 3:
                            wkr_used[slot] = v
                grp_end = self.sPE.n
                for j in range(4):
                    bank = 4 + j
                    hd = mg * 4 + j
                    ks = krst.ap(c0=j * TOK, c1=(j + 1) * TOK)
                    ev = self._rope(bank, ks, csbuf, TOK, grp_end,
                                    extra=((kr_slot_store[j],)
                                           if kr_slot_store[j] else ()) + (vcs,))
                    kr_slot_store[j] = sr.batch(
                        [(self.Ksc.ap()[C + hd * 128:C + (hd + 1) * 128,
                                        cs:cs + TOK], ks)],
                        waits=((self.sDV, ev),))
            h_used_prev = self.sPE.n

        self.settle()
        A.reset(self.base_keep)
        hbuf = A.take(FC * TOK)
        wdq = A.take(FC * C)
        wqr = A.take(CCH * E)
        tT = A.take(CCH * TOK)
        cq = A.take(CCH * TOK)
        qst = A.take(4 * TOK)
        csbuf = A.take(4 * TOK)
        self.ropet = A.take(2 * TOK)
        self.ropetmp = A.take(TOK)

        self.load16(wdq, self.w_d_qT)
        for g in range(4):
            src = bass.AP(self.w_qr_pT, g * 128 * E, [[E, 128], [1, E]])
            ld(wqr.ap(c0=g * E, c1=(g + 1) * E), src)
        vdq = vqr = self.sLD.n
        cst = self.stream("csq", "sync", 1)
        hb = self.stream("hq", "sync", 1)
        sq = self.stream("sq", "gpsimd", 2)

        h_used_prev = 0
        q_slot_store = [None] * 4
        for ci in range(self.NQ):
            cs = ci * TOK
            vx = hb.batch(
                [(hbuf.ap(c0=fc * TOK, c1=(fc + 1) * TOK),
                  self.xT_own.ap()[fc * 128:(fc + 1) * 128, cs:cs + TOK])
                 for fc in range(FC)],
                waits=((self.sPE, h_used_prev),))
            vcs = cst.batch(
                [(csbuf.ap(c0=b4 * TOK, c1=(b4 + 1) * TOK),
                  self.cs_q.ap()[:, b4 * self.So + cs:b4 * self.So + cs + TOK])
                 for b4 in range(4)],
                waits=((self.sDV, self._rope_dv_done),))
            ready = (vx,)
            h_ready = self.ln_mod(hbuf, TOK, self.ca1_sb, self.sa_sb, ready=ready)
            t_ev = 0
            for cc in range(CCH):
                bank = 4 + (cc % 2)
                parts = [(wdq.ap(c0=kc * C + cc * 128, c1=kc * C + cc * 128 + 128),
                          hbuf.ap(c0=kc * TOK, c1=(kc + 1) * TOK))
                         for kc in range(FC)]
                tok = self.mmgrp(bank, parts,
                                 waits=((self.sLD, vdq), (self.sDV, h_ready)))
                t_ev = dv(lambda e, o=tT.ap(c0=cc * TOK, c1=(cc + 1) * TOK),
                          i=self.P[bank].ap(): e.tensor_copy(o, i),
                          waits=((self.sPE, tok),))
                self.pfree[bank] = (self.sDV, t_ev)
            for mc in range(CCH):
                bank = 4 + (mc % 2)
                parts = [(self.wqe.ap(c0=kc * C + mc * 128,
                                      c1=kc * C + mc * 128 + 128),
                          tT.ap(c0=kc * TOK, c1=(kc + 1) * TOK))
                         for kc in range(CCH)]
                tok = self.mmgrp(bank, parts, waits=((self.sDV, t_ev),))
                slot = mc % 4
                ev = dv(lambda e, o=qst.ap(c0=slot * TOK, c1=(slot + 1) * TOK),
                        i=self.P[bank].ap(): e.tensor_copy(o, i),
                        waits=((self.sPE, tok),) +
                              ((q_slot_store[slot],) if q_slot_store[slot] else ()))
                self.pfree[bank] = (self.sDV, ev)
                q_slot_store[slot] = sq.batch(
                    [(self.Qsc.ap()[mc * 128:(mc + 1) * 128, cs:cs + TOK],
                      qst.ap(c0=slot * TOK, c1=(slot + 1) * TOK))],
                    waits=((self.sDV, ev),))
            src = [tT.ap(c0=i * TOK, c1=(i + 1) * TOK) for i in range(CCH)]
            dst = [cq.ap(c0=i * TOK, c1=(i + 1) * TOK) for i in range(CCH)]
            cq_done = self.rms_apply(src, dst, CCH, TOK, self.g1_sb, C, 6, 7,
                                     ready=((self.sDV, t_ev),))
            for hd in range(FC):
                bank = 4 + (hd % 2)
                parts = [(wqr.ap(c0=kc * E + hd * 128, c1=kc * E + hd * 128 + 128),
                          cq.ap(c0=kc * TOK, c1=(kc + 1) * TOK))
                         for kc in range(CCH)]
                tok = self.mmgrp(bank, parts,
                                 waits=((self.sLD, vqr), (self.sAC, cq_done)))
                slot = hd % 4
                ks = qst.ap(c0=slot * TOK, c1=(slot + 1) * TOK)
                ev = self._rope(bank, ks, csbuf, TOK, tok,
                                extra=((q_slot_store[slot],)
                                       if q_slot_store[slot] else ()) + (vcs,))
                q_slot_store[slot] = sq.batch(
                    [(self.Qsc.ap()[C + hd * 128:C + (hd + 1) * 128, cs:cs + TOK],
                      ks)], waits=((self.sDV, ev),))
            h_used_prev = self.sPE.n

    def phase_c(self):
        A = self.arena
        ld, st, dv, ac = self.ld, self.st, self.dv, self.ac
        self.settle()
        A.reset(self.base_keep)
        qbuf = A.take(QC * TOK)
        kbuf = A.take(2 * QC * TOK)
        vbuf = A.take(CCH * TOK)
        abuf = A.take(4 * TOK)
        atb = A.take(4 * TOK)
        obuf = A.take(4 * TOK)
        ubuf = Reg(self.sb, kbuf.c0, FC * TOK)
        xbuf = Reg(self.sb, kbuf.c0 + FC * TOK, FC * TOK)
        otb = Reg(self.sb, kbuf.c0 + 2 * FC * TOK, CCH * TOK)
        assert 2 * FC * TOK + CCH * TOK <= 2 * QC * TOK

        sm = self.small
        m_run = sm.ap(c0=0, c1=4)
        l_run = sm.ap(c0=4, c1=8)
        fcol = sm.ap(c0=8, c1=12)
        mold = sm.ap(c0=12, c1=16)
        mx = sm.ap(c0=16, c1=20)
        ps_ = sm.ap(c0=20, c1=24)
        negm = sm.ap(c0=24, c1=25)

        NKB = self.Sf // TOK
        qst_ = self.stream("cq", "sync", 1)
        kst = self.stream("ck", "sync", 2)
        vst = self.stream("cv", "sync", 2)
        xst = self.stream("cx", "sync", 1)
        sx = self.stream("sx", "gpsimd", 1)
        qq_done = None
        for qq in range(self.NQ):
            qs = qq * TOK
            wq0 = (qq_done or ())
            vq = qst_.batch(
                [(qbuf.ap(c0=t * TOK, c1=(t + 1) * TOK),
                  self.Qsc.ap()[t * 128:(t + 1) * 128, qs:qs + TOK])
                 for t in range(QC)], waits=wq0)
            k_used = [0, 0]
            v_used = 0
            for kb in range(NKB):
                ks = kb * TOK
                slot = kb % 2
                kslot = Reg(self.sb, kbuf.c0 + slot * QC * TOK, QC * TOK)
                w = ((self.sPE, k_used[slot]),) if kb >= 2 else wq0
                vk = kst.batch(
                    [(kslot.ap(c0=t * TOK, c1=(t + 1) * TOK),
                      self.Ksc.ap()[t * 128:(t + 1) * 128, ks:ks + TOK])
                     for t in range(QC)], waits=w)
                vsrc = bass.AP(self.Vsc, ks * C, [[C, 128], [128 * C, 4], [1, C]])
                wv = ((self.sPE, v_used),) if kb >= 1 else wq0
                vv = vst.batch([(vbuf.ap(), vsrc)], waits=wv)
                a_ready = []
                for qb in range(4):
                    bank = qb % 2
                    parts = [(qbuf.ap(c0=t * TOK + qb * 128,
                                      c1=t * TOK + qb * 128 + 128),
                              kslot.ap(c0=t * TOK, c1=(t + 1) * TOK))
                             for t in range(QC)]
                    tok = self.mmgrp(bank, parts, waits=(vq, vk))
                    if qb == 3:
                        k_used[slot] = self.sPE.n
                    q1 = lambda apx, qb=qb: apx[:, qb:qb + 1]
                    dv(lambda e, o=q1(mx), i=self.P[bank].ap():
                       e.reduce_max(o, i, axis=AX.X), waits=((self.sPE, tok),))
                    if kb == 0:
                        dv(lambda e, o=q1(m_run), i=q1(mx): e.tensor_copy(o, i))
                    else:
                        dv(lambda e, o=q1(mold), i=q1(m_run): e.tensor_copy(o, i))
                        dv(lambda e, o=q1(m_run), a=q1(m_run), b=q1(mx):
                           e.tensor_max(o, a, b))
                        tdf = dv(lambda e, o=q1(mold), a=q1(mold), b=q1(m_run):
                                 e.tensor_sub(o, a, b))
                        ac(lambda e, o=q1(fcol), i=q1(mold):
                           e.activation(o, i, AF.Exp),
                           waits=((self.sDV, tdf),))
                    tneg = dv(lambda e, o=negm, i=q1(m_run):
                              e.tensor_scalar_mul(o, i, -1.0))
                    texp = ac(lambda e, o=abuf.ap(c0=qb * TOK, c1=(qb + 1) * TOK),
                              i=self.P[bank].ap(), b=negm:
                              e.activation(o, i, AF.Exp, bias=b),
                              waits=((self.sDV, tneg),))
                    self.pfree[bank] = (self.sAC, texp)
                    dv(lambda e, o=q1(ps_),
                       i=abuf.ap(c0=qb * TOK, c1=(qb + 1) * TOK):
                       e.reduce_sum(o, i, axis=AX.X), waits=((self.sAC, texp),))
                    if kb == 0:
                        dv(lambda e, o=q1(l_run), i=q1(ps_): e.tensor_copy(o, i))
                    else:
                        dv(lambda e, o=q1(l_run), i=q1(l_run), s=q1(fcol),
                           p=q1(ps_):
                           e.scalar_tensor_tensor(o, i, s, p, ALU.mult, ALU.add),
                           waits=((self.sAC, self.sAC.n),))
                    for kc in range(4):
                        tb = 2 + (kc % 2)
                        tok2 = self.transp(tb,
                                           abuf.ap(c0=qb * TOK + kc * 128,
                                                   c1=qb * TOK + kc * 128 + 128),
                                           self.ident_sb.ap(),
                                           waits=((self.sAC, texp),))
                        ev = dv(lambda e,
                                o=atb.ap(c0=kc * TOK + qb * 128,
                                         c1=kc * TOK + (qb + 1) * 128),
                                i=self.P[tb].ap()[:, 0:128]: e.tensor_copy(o, i),
                                waits=((self.sPE, tok2),))
                        self.pfree[tb] = (self.sDV, ev)
                    a_ready.append(self.sDV.n)
                for qb in range(4):
                    bank = 4 + (qb % 2)
                    parts = [(atb.ap(c0=kc * TOK + qb * 128,
                                     c1=kc * TOK + (qb + 1) * 128),
                              vbuf.ap(c0=kc * TOK, c1=(kc + 1) * TOK))
                             for kc in range(4)]
                    tok = self.mmgrp(bank, parts,
                                     waits=((self.sDV, a_ready[qb]), vv))
                    if qb == 3:
                        v_used = self.sPE.n
                    ob = obuf.ap(c0=qb * TOK, c1=(qb + 1) * TOK)
                    if kb == 0:
                        ev = dv(lambda e, o=ob, i=self.P[bank].ap():
                                e.tensor_copy(o, i), waits=((self.sPE, tok),))
                    else:
                        ev = dv(lambda e, o=ob, i=ob, s=fcol[:, qb:qb + 1],
                                p=self.P[bank].ap():
                                e.scalar_tensor_tensor(o, i, s, p, ALU.mult,
                                                       ALU.add),
                                waits=((self.sPE, tok),))
                    self.pfree[bank] = (self.sDV, ev)
            scores_pe_done = self.sPE.n
            rl = sm.ap(c0=28, c1=32)
            dv(lambda e, o=rl, i=l_run: e.reciprocal(o, i))
            for qb in range(4):
                ob = obuf.ap(c0=qb * TOK, c1=(qb + 1) * TOK)
                dv(lambda e, o=ob, i=ob, s=rl[:, qb:qb + 1]:
                   e.tensor_scalar(o, i, s, None, ALU.mult))
            o_done = self.sDV.n
            for qb in range(4):
                for cc in range(CCH):
                    tb = 2 + ((qb * CCH + cc) % 2)
                    tok = self.transp(tb, obuf.ap(c0=qb * TOK + cc * 128,
                                                  c1=qb * TOK + cc * 128 + 128),
                                      self.ident_sb.ap(),
                                      waits=((self.sDV, o_done),))
                    ev = dv(lambda e, o=otb.ap(c0=cc * TOK + qb * 128,
                                               c1=cc * TOK + (qb + 1) * 128),
                            i=self.P[tb].ap()[:, 0:128]: e.tensor_copy(o, i),
                            waits=((self.sPE, tok),))
                    self.pfree[tb] = (self.sDV, ev)
            ot_done = self.sDV.n
            u_ev = 0
            for mc in range(FC):
                bank = 4 + (mc % 2)
                parts = [(self.awo.ap(c0=cc * E + mc * 128,
                                      c1=cc * E + mc * 128 + 128),
                          otb.ap(c0=cc * TOK, c1=(cc + 1) * TOK))
                         for cc in range(CCH)]
                tok = self.mmgrp(bank, parts, waits=((self.sDV, ot_done),))
                u_ev = dv(lambda e, o=ubuf.ap(c0=mc * TOK, c1=(mc + 1) * TOK),
                          i=self.P[bank].ap(): e.tensor_copy(o, i),
                          waits=((self.sPE, tok),))
                self.pfree[bank] = (self.sDV, u_ev)
            for i in range(FC):
                slot = i % 2
                tt, sq = self.sq_tmp(slot, ubuf.ap(c0=i * TOK, c1=(i + 1) * TOK),
                                     TOK, extra=((self.sDV, u_ev),) if i < 2 else ())
                w = self.puse(0) if i == 0 else ()
                v = self.E("tensor", lambda e, o=self.P[0].ap()[0:1, :],
                           l=self.ones_sb.ap(c0=0, c1=1), r=tt, a=(i == 0),
                           b=(i == FC - 1): e.matmul(o, l, r, start=a, stop=b),
                           w + ((self.sAC, sq),), inc=self.sPE)
                self._tmp_used[slot] = (self.sPE, v)
            rrow = self.stats.ap(p0=0, p1=1, c0=TOK, c1=2 * TOK)
            trs = ac(lambda e, o=rrow, i=self.P[0].ap()[0:1, :]:
                     e.activation(o, i, AF.Sqrt, scale=1.0 / E,
                                  bias=self.eps_sb.ap(p0=0, p1=1, c0=1, c1=2)),
                     waits=((self.sPE, self.sPE.n),))
            trc = dv(lambda e, o=rrow, i=rrow: e.reciprocal(o, i),
                     waits=((self.sAC, trs),))
            self.pfree[0] = (self.sAC, trs)
            tR = self.mmgrp(1, [(self.ones_sb.ap(p0=0, p1=1), rrow)],
                            waits=((self.sDV, trc),))
            vx = xst.batch(
                [(xbuf.ap(c0=fc * TOK, c1=(fc + 1) * TOK),
                  self.xT_own.ap()[fc * 128:(fc + 1) * 128, qs:qs + TOK])
                 for fc in range(FC)],
                waits=((self.sPE, scores_pe_done),))
            last = 0
            for fc in range(FC):
                u = ubuf.ap(c0=fc * TOK, c1=(fc + 1) * TOK)
                x = xbuf.ap(c0=fc * TOK, c1=(fc + 1) * TOK)
                dv(lambda e, o=u, i=u, r=self.P[1].ap(): e.tensor_mul(o, i, r),
                   waits=((self.sPE, tR), vx))
                last = dv(lambda e, o=x, i=u, s=self.gg_sb.ap(c0=fc, c1=fc + 1),
                          xx=x: e.scalar_tensor_tensor(o, i, s, xx, ALU.mult,
                                                       ALU.add))
            sxt = sx.batch(
                [(self.XN.ap()[fc * 128:(fc + 1) * 128, qs:qs + TOK],
                  xbuf.ap(c0=fc * TOK, c1=(fc + 1) * TOK)) for fc in range(FC)],
                waits=((self.sDV, last),))
            self.pfree[1] = (self.sDV, last)
            qq_done = ((self.sPE, self.sPE.n), (self.sDV, self.sDV.n),
                       (self.sAC, self.sAC.n), sxt)

    def phase_d(self):
        A = self.arena
        ld, st, dv, ac = self.ld, self.st, self.dv, self.ac
        self.settle()
        A.reset(self.hdr_keep)
        xn = A.take(FC * TOK)
        h2 = A.take(FC * TOK)
        xs = A.take(FC * TOK)
        racc = A.take(FC * TOK)
        zbuf = A.take(HCH * TOK)
        w13 = A.take(2 * 2 * FC * 128)
        w2b = A.take(2 * HCH * 128)
        scg = A.take(TOK)
        wsb = A.take(TOK)
        vt = A.take(4 * NEXP)
        sct = A.take(4 * NEXP)
        wt_ = A.take(4 * NEXP)
        g2_ = A.take(96)

        xnst = self.stream("dx", "sync", 1)
        w13st = self.stream("w13", "sync", 2)
        w2st = self.stream("w2", "sync", 2)
        so = self.stream("so", "gpsimd", 1)
        w13_used = [0, 0]
        w2_used = [0, 0]
        xn_prev = None
        for ci in range(self.NQ):
            cs = ci * TOK
            vx = xnst.batch(
                [(xn.ap(c0=fc * TOK, c1=(fc + 1) * TOK),
                  self.XN.ap()[fc * 128:(fc + 1) * 128, cs:cs + TOK])
                 for fc in range(FC)],
                waits=((xn_prev,) if xn_prev else ()))
            cp_done = 0
            for fc in range(FC):
                cp_done = dv(lambda e, o=h2.ap(c0=fc * TOK, c1=(fc + 1) * TOK),
                             i=xn.ap(c0=fc * TOK, c1=(fc + 1) * TOK):
                             e.tensor_copy(o, i), waits=(vx,))
            h2_ready = self.ln_mod(h2, TOK, self.cb1_sb, self.sb_sb,
                                   ready=((self.sDV, cp_done),))

            parts = [(self.gatew_sb.ap(c0=kc * NEXP, c1=(kc + 1) * NEXP),
                      h2.ap(c0=kc * TOK, c1=(kc + 1) * TOK)) for kc in range(FC)]
            tok = self.mmgrp(0, parts, waits=((self.sDV, h2_ready),))
            tsg = ac(lambda e, o=scg.ap(p0=0, p1=NEXP),
                     i=self.P[0].ap()[0:NEXP, :]:
                     e.activation(o, i, AF.Sigmoid), waits=((self.sPE, tok),))
            self.pfree[0] = (self.sAC, tsg)
            for ts in range(4):
                tb = 2 + (ts % 2)
                tok2 = self.transp(tb, scg.ap(p0=0, p1=NEXP, c0=ts * 128,
                                              c1=(ts + 1) * 128),
                                   self.ident_sb.ap(p0=0, p1=NEXP, c0=0, c1=NEXP),
                                   waits=((self.sAC, tsg),))
                sc_t = sct.ap(c0=ts * NEXP, c1=(ts + 1) * NEXP)
                v_t = vt.ap(c0=ts * NEXP, c1=(ts + 1) * NEXP)
                dv(lambda e, o=sc_t, i=self.P[tb].ap()[:, 0:NEXP]:
                   e.tensor_copy(o, i), waits=((self.sPE, tok2),))
                ev = dv(lambda e, o=v_t, a=self.P[tb].ap()[:, 0:NEXP],
                        b=self.biasbc_sb.ap(): e.tensor_add(o, a, b))
                self.pfree[tb] = (self.sDV, ev)
                self._top2(v_t, sc_t, wt_.ap(c0=ts * NEXP, c1=(ts + 1) * NEXP), g2_)
            for ts in range(4):
                tb = 2 + (ts % 2)
                tok2 = self.transp(tb, wt_.ap(c0=ts * NEXP, c1=(ts + 1) * NEXP),
                                   self.ident_sb.ap(),
                                   waits=((self.sDV, self.sDV.n),))
                ev = dv(lambda e, o=wsb.ap(p0=0, p1=NEXP, c0=ts * 128,
                                           c1=(ts + 1) * 128),
                        i=self.P[tb].ap()[0:NEXP, 0:128]: e.tensor_copy(o, i),
                        waits=((self.sPE, tok2),))
                self.pfree[tb] = (self.sDV, ev)
            w_ready = self.sDV.n

            for ei in range(NEXP + 1):
                e_ = ei - 1
                if ei == 0:
                    xsrc = h2
                    xs_tok = (self.sDV, h2_ready)
                else:
                    bb = 6 + (ei % 2)
                    tok = self.mmgrp(bb, [(self.sel8_sb.ap(p0=0, p1=NEXP,
                                                             c0=e_ * 128,
                                                             c1=(e_ + 1) * 128),
                                           wsb.ap(p0=0, p1=NEXP))],
                                     waits=((self.sDV, w_ready),))
                    lastx = 0
                    for fc in range(FC):
                        lastx = dv(lambda e, o=xs.ap(c0=fc * TOK, c1=(fc + 1) * TOK),
                                   a=h2.ap(c0=fc * TOK, c1=(fc + 1) * TOK),
                                   b=self.P[bb].ap(): e.tensor_mul(o, a, b),
                                   waits=((self.sPE, tok),))
                    self.pfree[bb] = (self.sDV, lastx)
                    xsrc = xs
                    xs_tok = (self.sDV, lastx)
                z_ev = 0
                for hc in range(HCH):
                    slot = hc % 2
                    wreg = Reg(self.sb, w13.c0 + slot * 2 * FC * 128, 2 * FC * 128)
                    if ei == 0:
                        o1, o3 = hc * 128 * E, hc * 128 * E
                        d1, d3 = self.sw1H, self.sw3H
                    else:
                        o1 = o3 = (e_ * HCH + hc) * 128 * E
                        d1, d3 = self.w1H, self.w3H
                    s1 = bass.AP(d1, o1, [[E, 128], [1, E]])
                    s3 = bass.AP(d3, o3, [[E, 128], [1, E]])
                    l3 = w13st.batch(
                        [(wreg.ap(c0=0, c1=FC * 128), s1),
                         (wreg.ap(c0=FC * 128, c1=2 * FC * 128), s3)],
                        waits=((self.sPE, w13_used[slot]),))
                    pb1, pb3 = (0, 1) if slot == 0 else (2, 3)
                    parts1 = [(wreg.ap(c0=kc * 128, c1=(kc + 1) * 128),
                               xsrc.ap(c0=kc * TOK, c1=(kc + 1) * TOK))
                              for kc in range(FC)]
                    parts3 = [(wreg.ap(c0=FC * 128 + kc * 128,
                                       c1=FC * 128 + (kc + 1) * 128),
                               xsrc.ap(c0=kc * TOK, c1=(kc + 1) * TOK))
                              for kc in range(FC)]
                    self.mmgrp(pb1, parts1, waits=(l3, xs_tok))
                    t3 = self.mmgrp(pb3, parts3)
                    w13_used[slot] = t3
                    tt, tz1 = None, None
                    ttap = (self.tmp1 if slot == 0 else self.tmp2).ap()
                    wz = ((self._tmp_used[slot],) if self._tmp_used[slot] else ())
                    tz1 = ac(lambda e, o=ttap, i=self.P[pb1].ap():
                             e.activation(o, i, AF.Silu),
                             waits=((self.sPE, t3),) + wz)
                    z_ev = dv(lambda e, o=zbuf.ap(c0=hc * TOK, c1=(hc + 1) * TOK),
                              a=ttap, b=self.P[pb3].ap(): e.tensor_mul(o, a, b),
                              waits=((self.sAC, tz1),))
                    self._tmp_used[slot] = (self.sDV, z_ev)
                    self.pfree[pb1] = (self.sAC, tz1)
                    self.pfree[pb3] = (self.sDV, z_ev)
                for oc in range(FC):
                    slot = oc % 2
                    w2reg = Reg(self.sb, w2b.c0 + slot * HCH * 128, HCH * 128)
                    if ei == 0:
                        s2 = bass.AP(self.sw2H, oc * 128 * HID, [[HID, 128], [1, HID]])
                    else:
                        s2 = bass.AP(self.w2H, (e_ * FC + oc) * 128 * HID,
                                     [[HID, 128], [1, HID]])
                    l2 = w2st.batch([(w2reg.ap(), s2)],
                                    waits=((self.sPE, w2_used[slot]),))
                    bank = 4 + slot
                    parts = [(w2reg.ap(c0=kc * 128, c1=(kc + 1) * 128),
                              zbuf.ap(c0=kc * TOK, c1=(kc + 1) * TOK))
                             for kc in range(HCH)]
                    tok = self.mmgrp(bank, parts,
                                     waits=(l2, (self.sDV, z_ev)))
                    w2_used[slot] = tok
                    r = racc.ap(c0=oc * TOK, c1=(oc + 1) * TOK)
                    if ei == 0:
                        ev = dv(lambda e, o=r, i=self.P[bank].ap():
                                e.tensor_copy(o, i), waits=((self.sPE, tok),))
                    else:
                        ev = dv(lambda e, o=r, a=r, b=self.P[bank].ap():
                                e.tensor_add(o, a, b), waits=((self.sPE, tok),))
                    self.pfree[bank] = (self.sDV, ev)
            last = 0
            for fc in range(FC):
                r = racc.ap(c0=fc * TOK, c1=(fc + 1) * TOK)
                x = xn.ap(c0=fc * TOK, c1=(fc + 1) * TOK)
                last = dv(lambda e, o=x, i=r, s=self.gbc_sb.ap(c0=fc, c1=fc + 1),
                          xx=x: e.scalar_tensor_tensor(o, i, s, xx, ALU.mult,
                                                       ALU.add))
            xn_prev = so.batch(
                [(self.outT.ap()[fc * 128:(fc + 1) * 128, cs:cs + TOK],
                  xn.ap(c0=fc * TOK, c1=(fc + 1) * TOK)) for fc in range(FC)],
                waits=((self.sDV, last),))

    def _top2(self, v_ap, sc_ap, w_ap, g2_):
        dv = self.dv
        a = lambda c0, c1: g2_.ap(c0=c0, c1=c1)
        vmax = a(0, 1)
        eq = a(8, 16)
        c2 = a(16, 24)
        c3 = a(24, 32)
        c4 = a(32, 40)
        iz = a(40, 48)
        m1 = a(48, 56)
        m2 = a(56, 64)
        v2 = a(64, 72)

        def first_max(vin, mout):
            dv(lambda e, o=vmax, i=vin: e.reduce_max(o, i, axis=AX.X))
            dv(lambda e, o=eq, i=vin, s=vmax:
               e.tensor_scalar(o, i, s, None, ALU.is_ge))
            dv(lambda e, o=c2[:, 0:1], i=eq[:, 0:1]: e.tensor_copy(o, i))
            dv(lambda e, o=c2[:, 1:8], x=eq[:, 1:8], y=eq[:, 0:7]:
               e.tensor_add(o, x, y))
            dv(lambda e, o=c3[:, 0:2], i=c2[:, 0:2]: e.tensor_copy(o, i))
            dv(lambda e, o=c3[:, 2:8], x=c2[:, 2:8], y=c2[:, 0:6]:
               e.tensor_add(o, x, y))
            dv(lambda e, o=c4[:, 0:4], i=c3[:, 0:4]: e.tensor_copy(o, i))
            dv(lambda e, o=c4[:, 4:8], x=c3[:, 4:8], y=c3[:, 0:4]:
               e.tensor_add(o, x, y))
            dv(lambda e, o=c4, x=c4, y=eq: e.tensor_sub(o, x, y))
            dv(lambda e, o=iz, i=c4: e.tensor_scalar(o, i, 0.5, None, ALU.is_le))
            dv(lambda e, o=mout, x=eq, y=iz: e.tensor_mul(o, x, y))

        first_max(v_ap, m1)
        dv(lambda e, o=v2, i=m1, x=v_ap:
           e.scalar_tensor_tensor(o, i, -1e5, x, ALU.mult, ALU.add))
        first_max(v2, m2)
        dv(lambda e, o=m1, x=m1, y=m2: e.tensor_add(o, x, y))
        dv(lambda e, o=w_ap, x=m1, y=sc_ap: e.tensor_mul(o, x, y))

    def build(self):
        self.phase_a()
        self.settle()
        self.phase_b()
        self.phase_c()
        self.phase_d()
        self.settle()

        nc = self.nc
        with nc.Block() as blk:
            for eng, attr in (("sync", blk.sync), ("gpsimd", blk.gpsimd),
                              ("tensor", blk.tensor), ("vector", blk.vector),
                              ("scalar", blk.scalar)):
                def run(e, eng=eng):
                    for f in self.q[eng]:
                        f(e)
                attr(run)
        for ctx in self._sem_ctxs:
            ctx.__exit__(None, None, None)
        nc.finalize()
        return nc



def _rope_tables(S, scale=1.0):
    D = 128
    cos = sin = None
    try:
        import jax
        import jax.numpy as jnp
        cpu = jax.devices("cpu")[0]
        with jax.default_device(cpu):
            freqs = 1.0 / 10000.0 ** (jnp.arange(0, D, 2, dtype=jnp.float32) / D)
            ang = jnp.arange(S, dtype=jnp.float32)[:, None] * freqs[None, :]
            cos = np.asarray(jax.device_get(jnp.cos(ang))).T
            sin = np.asarray(jax.device_get(jnp.sin(ang))).T
    except Exception:
        pass
    if cos is None:
        e32 = (np.arange(0, D, 2).astype(np.float32) / np.float32(D))
        freqs = np.float32(1.0) / (np.float64(10000.0)
                                   ** e32.astype(np.float64)).astype(np.float32)
        ang = (np.arange(S, dtype=np.float32)[:, None]
               * freqs[None, :]).astype(np.float32)
        cos = np.cos(ang.astype(np.float64)).astype(np.float32).T
        sin = np.sin(ang.astype(np.float64)).astype(np.float32).T
    A = np.concatenate([cos, cos], axis=0).astype(np.float64)
    B = np.concatenate([-sin, sin], axis=0).astype(np.float64)
    A *= scale
    B *= scale
    Ahi = A.astype(np.float32)
    Alo = (A - Ahi).astype(np.float32)
    Bhi = B.astype(np.float32)
    Blo = (B - Bhi).astype(np.float32)
    return np.concatenate([Ahi, Alo, Bhi, Blo], axis=1)


def _head_perm():
    p = np.empty(E, dtype=np.int64)
    for h in range(16):
        base = h * 128
        p[base:base + 64] = base + np.arange(0, 128, 2)
        p[base + 64:base + 128] = base + np.arange(1, 128, 2)
    return p


def _tile_w13(w):
    n = w.shape[0]
    wt = w.transpose(0, 2, 1).reshape(n, FC, 128, HCH, 128)
    return np.ascontiguousarray(wt.transpose(0, 3, 2, 1, 4)).reshape(n * HCH * 128, E)


def _tile_w2(w):
    n = w.shape[0]
    wt = w.transpose(0, 2, 1).reshape(n, HCH, 128, FC, 128)
    return np.ascontiguousarray(wt.transpose(0, 3, 2, 1, 4)).reshape(n * FC * 128, HID)


def prep_inputs(inputs, S_full, S_own, cores):
    cc = np.ascontiguousarray
    perm = _head_perm()
    x = np.asarray(inputs["L0_attn_h"], dtype=np.float32)
    cs_k = cc(_rope_tables(S_full))
    cs_q_full = _rope_tables(S_full, scale=1.0 / 12.0)
    swp = np.zeros((128, 128), dtype=np.float32)
    for i in range(128):
        swp[i, i ^ 64] = 1.0

    shared = {
        "wmod": cc(np.asarray(inputs["L0_wmod"], np.float32)),
        "w_d_kvT": cc(np.asarray(inputs["L0_attn_w_d_kv"], np.float32).T),
        "w_d_qT": cc(np.asarray(inputs["L0_attn_w_d_q"], np.float32).T),
        "w_u_q": cc(np.asarray(inputs["L0_attn_w_u_q"], np.float32)),
        "w_u_k": cc(np.asarray(inputs["L0_attn_w_u_k"], np.float32)),
        "w_u_v": cc(np.asarray(inputs["L0_attn_w_u_v"], np.float32)),
        "w_oT": cc(np.asarray(inputs["L0_attn_w_o"], np.float32).T),
        "w_qr_pT": cc(np.asarray(inputs["L0_attn_w_qr"], np.float32)[perm].T),
        "w_kr_pT": cc(np.asarray(inputs["L0_attn_w_kr"], np.float32)[perm].T),
        "g1c": cc(np.asarray(inputs["L0_attn_g1"], np.float32).reshape(CCH, 128).T),
        "g2c": cc(np.asarray(inputs["L0_attn_g2"], np.float32).reshape(CCH, 128).T),
        "g3c": cc(np.asarray(inputs["L0_attn_g3"], np.float32).reshape(FC, 128).T),
        "gate_wT": cc(np.asarray(inputs["L0_ffn_gate_w"], np.float32).T),
        "bias_bc": cc(np.broadcast_to(
            np.asarray(inputs["L0_ffn_expert_bias"], np.float32)[None, :],
            (128, NEXP)).copy()),
        "sw1H": _tile_w13(np.asarray(inputs["L0_ffn_sw1"], np.float32)[None]),
        "sw3H": _tile_w13(np.asarray(inputs["L0_ffn_sw3"], np.float32)[None]),
        "sw2H": _tile_w2(np.asarray(inputs["L0_ffn_sw2"], np.float32)[None]),
        "w1H": _tile_w13(np.asarray(inputs["L0_ffn_w1"], np.float32)),
        "w3H": _tile_w13(np.asarray(inputs["L0_ffn_w3"], np.float32)),
        "w2H": _tile_w2(np.asarray(inputs["L0_ffn_w2"], np.float32)),
        "cs_k": cs_k,
        "ident": np.eye(128, dtype=np.float32),
        "ones": np.ones((128, 128), dtype=np.float32),
        "epsc": np.broadcast_to(np.array([EPS_LN, EPS_RMS], np.float32)[None, :],
                                (128, 2)).copy(),
        "swp": swp,
        "sel8": np.concatenate(
            [np.where(np.arange(NEXP)[:, None] == e_,
                      np.float32(1.0), np.float32(0.0)).repeat(128, axis=1)
             for e_ in range(NEXP)], axis=1).astype(np.float32),
    }
    cmat = np.asarray(inputs["L0_c"], np.float32)
    in_maps = []
    for core in range(cores):
        b = (core // 2) % x.shape[0]
        h = core % 2
        xT = cc(x[b].T)
        own = slice(h * S_own, (h + 1) * S_own)
        m = dict(shared)
        m["xT_full"] = xT
        m["xT_own"] = cc(xT[:, own])
        m["cT"] = cc(cmat[b].reshape(FC, 128).T)
        m["cs_q"] = cc(np.concatenate(
            [cs_q_full[:, b4 * S_full + h * S_own:b4 * S_full + (h + 1) * S_own]
             for b4 in range(4)], axis=1))
        in_maps.append(m)
    return in_maps


def kernel(**inputs):
    x = np.asarray(inputs["L0_attn_h"])
    B, S_full, _ = x.shape
    n_cores = 8
    S_own = S_full * B // n_cores
    kb = KB(S_full, S_own)
    nc = kb.build()
    in_maps = prep_inputs(inputs, S_full, S_own, n_cores)
    res = run_bass_kernel_spmd(nc, in_maps, core_ids=list(range(n_cores)))
    out = np.empty((B, S_full, E), dtype=np.float32)
    for core in range(n_cores):
        b = core // 2
        h = core % 2
        out[b, h * S_own:(h + 1) * S_own, :] = \
            np.asarray(res.results[core]["outT"]).T
    return out


# version 07
# speedup vs baseline: 1.0648x; 1.0381x over previous
import numpy as np
import concourse.bass as bass
import concourse.mybir as mybir
from concourse.bass_utils import run_bass_kernel_spmd

F32 = mybir.dt.float32
AF = mybir.ActivationFunctionType
ALU = mybir.AluOpType
AX = mybir.AxisListType

EPS_RMS = float(np.finfo(np.float64).eps)
EPS_LN = 1e-5
SMX_HI = float(np.float32(1.0 / 12.0))
SMX_LO = float(np.float32(1.0 / 12.0 - np.float64(np.float32(1.0 / 12.0))))
E = 2048
C = 512
HID = 1024
NEXP = 8
FC = E // 128
CCH = C // 128
HCH = HID // 128
TOK = 512
QC = (C + E) // 128
ENGS = ("sync", "gpsimd", "tensor", "vector", "scalar")


class Sem:
    __slots__ = ("h", "n", "step")

    def __init__(self, h, step):
        self.h, self.n, self.step = h, 0, step


class Reg:
    __slots__ = ("t", "c0", "ncols")

    def __init__(self, t, c0, ncols):
        self.t, self.c0, self.ncols = t, c0, ncols

    def ap(self, p0=0, p1=128, c0=0, c1=None):
        if c1 is None:
            c1 = self.ncols
        return self.t.ap()[p0:p1, self.c0 + c0:self.c0 + c1]


class Arena:
    def __init__(self, t, ncols):
        self.t, self.n, self.off = t, ncols, 0

    def take(self, ncols):
        r = Reg(self.t, self.off, ncols)
        self.off += ncols
        assert self.off <= self.n, f"SBUF arena overflow: {self.off} > {self.n}"
        return r

    def reset(self, keep=0):
        self.off = keep


class Stream:

    def __init__(self, kb, name, eng, nsems):
        self.kb, self.eng = kb, eng
        self.sems = [kb._new_sem(f"{name}{i}", 16) for i in range(nsems)]
        self.i = 0

    def batch(self, transfers, waits=()):
        kb = self.kb
        sem = self.sems[self.i % len(self.sems)]
        self.i += 1
        kb.E(self.eng, None, waits=((sem, sem.n),) + tuple(waits))
        for dst, src in transfers:
            kb.E(self.eng, lambda e, d=dst, s=src: e.dma_start(out=d, in_=s),
                 (), inc=sem)
        return (sem, sem.n)


class KB:
    def __init__(self, S_full, S_own, debug=False):
        nc = bass.Bass("TRN2", target_bir_lowering=False)
        self.nc = nc
        self.Sf, self.So = S_full, S_own
        self.NKV = S_full // TOK
        self.NQ = S_own // TOK
        self.q = {e: [] for e in ENGS}
        self.hw = {e: {} for e in ENGS}
        self._sem_ctxs = []

        d = lambda n, s: nc.dram_tensor(n, s, F32, kind="ExternalInput")
        self.xT_full = d("xT_full", [E, S_full])
        self.xT_own = d("xT_own", [E, S_own])
        self.cT = d("cT", [128, FC])
        self.wmod = d("wmod", [E, 6 * E])
        self.w_d_kvT = d("w_d_kvT", [E, C])
        self.w_d_qT = d("w_d_qT", [E, C])
        self.w_u_q = d("w_u_q", [E, C])
        self.w_u_k = d("w_u_k", [E, C])
        self.w_u_v = d("w_u_v", [E, C])
        self.w_oT = d("w_oT", [E, E])
        self.w_qr_pT = d("w_qr_pT", [C, E])
        self.w_kr_pT = d("w_kr_pT", [E, E])
        self.g1c = d("g1c", [128, CCH])
        self.g2c = d("g2c", [128, CCH])
        self.g3c = d("g3c", [128, FC])
        self.gate_wT = d("gate_wT", [E, NEXP])
        self.bias_bc = d("bias_bc", [128, NEXP])
        self.sw1H = d("sw1H", [HCH * 128, E])
        self.sw3H = d("sw3H", [HCH * 128, E])
        self.sw2H = d("sw2H", [FC * 128, HID])
        self.w1H = d("w1H", [NEXP * HCH * 128, E])
        self.w3H = d("w3H", [NEXP * HCH * 128, E])
        self.w2H = d("w2H", [NEXP * FC * 128, HID])
        self.cs_k = d("cs_k", [128, 4 * S_full])
        self.cs_q = d("cs_q", [128, 4 * S_own])
        self.ident = d("ident", [128, 128])
        self.ones = d("ones", [128, 128])
        self.sel8 = d("sel8", [NEXP, NEXP * 128])
        self.epsc = d("epsc", [128, 2])
        self.swp = d("swp", [128, 128])

        self.outT = nc.dram_tensor("outT", [E, S_own], F32, kind="ExternalOutput")
        skind = "ExternalOutput" if debug else "Internal"
        self.Ksc = nc.dram_tensor("Ksc", [C + E, S_full], F32, kind=skind)
        self.Vsc = nc.dram_tensor("Vsc", [S_full, C], F32, kind=skind)
        self.Qsc = nc.dram_tensor("Qsc", [C + E, S_own], F32, kind=skind)
        self.XN = nc.dram_tensor("XN", [E, S_own], F32, kind=skind)

        self.all_sems = []
        for name, step in (("sLD", 16), ("sST", 16), ("sPE", 1), ("sDV", 1),
                           ("sAC", 1)):
            setattr(self, name, self._new_sem(name, step))

        self.P = [nc.alloc_psum_tensor(f"ps{i}", [128, TOK], F32) for i in range(8)]
        self.grnd = nc.alloc_sbuf_tensor("grnd", [128, TOK], mybir.dt.int32)
        self.grnd_f = self.grnd.bitcast(F32)
        self.pfree = [None] * 8

        total = nc.sbuf_bytes_remaining // 4 - 64
        self.sb = nc.alloc_sbuf_tensor("arena", [128, total], F32)
        A = Arena(self.sb, total)
        self.ident_sb = A.take(128)
        self.ones_sb = A.take(128)
        self.eps_sb = A.take(2)
        self.swp_sb = A.take(128)
        self.g1_sb = A.take(CCH)
        self.g2_sb = A.take(CCH)
        self.g3_sb = A.take(FC)
        self.cT_sb = A.take(FC)
        self.siluc = A.take(FC)
        self.gatew_sb = A.take(FC * NEXP)
        self.biasbc_sb = A.take(NEXP)
        self.mod_sb = A.take(96)
        self.sa_sb = A.take(FC)
        self.ca1_sb = A.take(FC)
        self.sb_sb = A.take(FC)
        self.cb1_sb = A.take(FC)
        self.gg_sb = A.take(FC)
        self.gbc_sb = A.take(FC)
        self.stats = A.take(2 * TOK)
        self.small = A.take(64)
        self.tmp1 = A.take(TOK)
        self.tmp2 = A.take(TOK)
        self.ropet = None
        self.clamp_t = None
        self.hdr_keep = A.off
        self.wqe = A.take(CCH * C)
        self.awo = A.take(CCH * E)
        self.arena = A
        self.base_keep = A.off

        self._tmp_used = [None, None]
        self._rope_x_used = [None, None]
        self._rope_slot = 0
        self._rope_dv_done = 0
        self._qq_done = None

    def _new_sem(self, name, step):
        ctx = self.nc.semaphore(name)
        s = Sem(ctx.__enter__(), step)
        self._sem_ctxs.append(ctx)
        self.all_sems.append(s)
        return s

    def stream(self, name, eng, nsems=2):
        return Stream(self, name, eng, nsems)

    def E(self, eng, fn, waits=(), inc=None):
        hw = self.hw[eng]
        for t in waits:
            if t is None:
                continue
            s, v = t
            if v > 0 and hw.get(id(s), 0) < v:
                hw[id(s)] = v
                self.q[eng].append(lambda e, h=s.h, vv=v: e.wait_ge(h, vv))
        if fn is None:
            return None
        if inc is None:
            self.q[eng].append(fn)
            return None
        inc.n += inc.step
        v = inc.n
        self.q[eng].append(lambda e, f=fn, h=inc.h, st=inc.step: f(e).then_inc(h, st))
        return v

    def ld(self, dst, src, waits=()):
        return self.E("sync", lambda e, d=dst, s=src: e.dma_start(out=d, in_=s),
                      waits, inc=self.sLD)

    def st(self, dst, src, waits=()):
        return self.E("gpsimd", lambda e, d=dst, s=src: e.dma_start(out=d, in_=s),
                      waits, inc=self.sST)

    def settle(self, engs=("tensor", "vector", "scalar", "sync", "gpsimd")):
        toks = [(s, s.n) for s in self.all_sems]
        for eng in engs:
            self.E(eng, None, waits=toks)

    def puse(self, bank):
        t = self.pfree[bank]
        return (t,) if t else ()

    def mmgrp(self, bank, parts, waits=(), inc_each=False):
        n = len(parts)
        p = self.P[bank].ap()
        toks = []
        for i, (l, r) in enumerate(parts):
            m = l.shape[-1]
            nn = r.shape[-1]
            a, b = (i == 0), (i == n - 1)
            w = (tuple(waits) + self.puse(bank)) if i == 0 else ()
            do_inc = inc_each or b
            v = self.E("tensor",
                       lambda e, o=p[0:m, 0:nn], ll=l, rr=r, a=a, b=b:
                       e.matmul(o, ll, rr, start=a, stop=b),
                       w, inc=(self.sPE if do_inc else None))
            toks.append(v if do_inc else None)
        return self.sPE.n if not inc_each else toks

    def transp(self, bank, in_ap, idn, waits=()):
        w = tuple(waits) + self.puse(bank)
        p = self.P[bank].ap()
        pr = in_ap.shape[-1]
        pc = in_ap.shape[0]
        self.E("tensor", lambda e, o=p[0:pr, 0:pc], i=in_ap, d=idn:
               e.transpose(o, i, d), w, inc=self.sPE)
        return self.sPE.n

    def dv(self, fn, waits=()):
        return self.E("vector", fn, ((self.sDV, self.sDV.n),) + tuple(waits),
                      inc=self.sDV)

    def ac(self, fn, waits=()):
        return self.E("scalar", fn, ((self.sAC, self.sAC.n),) + tuple(waits),
                      inc=self.sAC)

    def sq_tmp(self, slot, src_ap, ntok, extra=()):
        tt = (self.tmp1 if slot == 0 else self.tmp2).ap(c0=0, c1=ntok)
        w = ((self._tmp_used[slot],) if self._tmp_used[slot] else ()) + tuple(extra)
        tok = self.ac(lambda e, o=tt, i=src_ap: e.activation(o, i, AF.Square), w)
        return tt, tok

    def ln_mod(self, hbuf, ntok, scale_sb, shift_sb, ready=()):
        dv, ac = self.dv, self.ac
        for fc in range(FC):
            x = hbuf.ap(c0=fc * TOK, c1=fc * TOK + ntok)
            w0 = (self.puse(0) if fc == 0 else ()) + (tuple(ready) if fc == 0 else ())
            self.E("tensor", lambda e, o=self.P[0].ap()[0:1, 0:ntok],
                   l=self.ones_sb.ap(c0=0, c1=1), r=x, a=(fc == 0), b=(fc == FC - 1):
                   e.matmul(o, l, r, start=a, stop=b), w0,
                   inc=(self.sPE if fc == FC - 1 else None))
            slot = fc % 2
            tt, sq = self.sq_tmp(slot, x, ntok, extra=(tuple(ready) if fc < 2 else ()))
            w1 = self.puse(1) if fc == 0 else ()
            v = self.E("tensor", lambda e, o=self.P[1].ap()[0:1, 0:ntok],
                       l=self.ones_sb.ap(c0=0, c1=1), r=tt, a=(fc == 0),
                       b=(fc == FC - 1): e.matmul(o, l, r, start=a, stop=b),
                       w1 + ((self.sAC, sq),), inc=self.sPE)
            self._tmp_used[slot] = (self.sPE, v)
        sums_done = self.sPE.n
        stp = lambda r: self.stats.ap(p0=0, p1=1, c0=r * TOK, c1=r * TOK + ntok)
        dv(lambda e, o=stp(0), i=self.P[0].ap()[0:1, 0:ntok]:
           e.tensor_scalar_mul(o, i, 1.0 / E), waits=((self.sPE, sums_done),))
        dv(lambda e, o=stp(1), a=stp(0): e.tensor_mul(o, a, a))
        tvar = dv(lambda e, o=stp(1), i=self.P[1].ap()[0:1, 0:ntok], m=stp(1):
                  e.scalar_tensor_tensor(o, i, 1.0 / E, m, ALU.mult, ALU.subtract))
        trs = ac(lambda e, o=stp(1), i=stp(1):
                 e.activation(o, i, AF.Sqrt,
                              bias=self.eps_sb.ap(p0=0, p1=1, c0=0, c1=1)),
                 waits=((self.sDV, tvar),))
        trc = dv(lambda e, o=stp(1), i=stp(1): e.reciprocal(o, i),
                 waits=((self.sAC, trs),))
        tmr = dv(lambda e, o=stp(0), a=stp(0), b=stp(1): e.tensor_mul(o, a, b))
        self.pfree[0] = (self.sDV, tmr)
        self.pfree[1] = (self.sDV, tvar)
        tR = self.mmgrp(2, [(self.ones_sb.ap(p0=0, p1=1), stp(1))],
                        waits=((self.sDV, trc),))
        tMR = self.mmgrp(3, [(self.ones_sb.ap(p0=0, p1=1), stp(0))],
                         waits=((self.sDV, tmr),))
        last_dv = 0
        for fc in range(FC):
            x = hbuf.ap(c0=fc * TOK, c1=fc * TOK + ntok)
            dv(lambda e, o=x, i=x, r=self.P[2].ap()[:, 0:ntok]: e.tensor_mul(o, i, r),
               waits=((self.sPE, tR),))
            last_dv = dv(lambda e, o=x, i=x, r=self.P[3].ap()[:, 0:ntok]:
                         e.tensor_sub(o, i, r), waits=((self.sPE, tMR),))
            last_dv = dv(lambda e, o=x, i=x, s=scale_sb.ap(c0=fc, c1=fc + 1),
                         b=shift_sb.ap(c0=fc, c1=fc + 1):
                         e.tensor_scalar(o, i, s, b, ALU.mult, ALU.add))
        self.pfree[2] = (self.sDV, last_dv)
        self.pfree[3] = (self.sDV, last_dv)
        return last_dv

    def rms_apply(self, src, dst, nch, ntok, g_sb, dim, bank_s, bank_b, ready=()):
        dv, ac = self.dv, self.ac
        for i in range(nch):
            slot = i % 2
            tt, sq = self.sq_tmp(slot, src[i], ntok,
                                 extra=(tuple(ready) if i < 2 else ()))
            w = self.puse(bank_s) if i == 0 else ()
            v = self.E("tensor", lambda e, o=self.P[bank_s].ap()[0:1, 0:ntok],
                       l=self.ones_sb.ap(c0=0, c1=1), r=tt, a=(i == 0),
                       b=(i == nch - 1): e.matmul(o, l, r, start=a, stop=b),
                       w + ((self.sAC, sq),), inc=self.sPE)
            self._tmp_used[slot] = (self.sPE, v)
        tok = self.sPE.n
        rrow = self.stats.ap(p0=0, p1=1, c0=TOK, c1=TOK + ntok)
        trs = ac(lambda e, o=rrow, i=self.P[bank_s].ap()[0:1, 0:ntok]:
                 e.activation(o, i, AF.Sqrt, scale=1.0 / dim,
                              bias=self.eps_sb.ap(p0=0, p1=1, c0=1, c1=2)),
                 waits=((self.sPE, tok),))
        trc = dv(lambda e, o=rrow, i=rrow: e.reciprocal(o, i),
                 waits=((self.sAC, trs),))
        self.pfree[bank_s] = (self.sAC, trs)
        tR = self.mmgrp(bank_b, [(self.ones_sb.ap(p0=0, p1=1), rrow)],
                        waits=((self.sDV, trc),))
        for i in range(nch):
            dv(lambda e, o=dst[i], a=src[i], r=self.P[bank_b].ap()[:, 0:ntok]:
               e.tensor_mul(o, a, r), waits=((self.sPE, tR),))
            self.ac(lambda e, o=dst[i], i_=dst[i], s=g_sb.ap(c0=i, c1=i + 1):
                    e.activation(o, i_, AF.Copy, scale=s),
                    waits=((self.sDV, self.sDV.n),))
        self.pfree[bank_b] = (self.sAC, self.sAC.n)
        return self.sAC.n

    def _rope(self, bank, out_ap, cs_reg, ntok, pe_tok, extra=()):
        dv = self.dv
        blk = lambda i: cs_reg.ap(c0=i * TOK, c1=i * TOK + ntok)
        slot = self._rope_slot
        self._rope_slot ^= 1
        xst = self.ropet.ap(c0=slot * TOK, c1=slot * TOK + ntok)
        tmp = self.ropetmp.ap(c0=0, c1=ntok)
        w = ((self.sPE, pe_tok),) + tuple(extra)
        if self._rope_x_used[slot] is not None:
            w = w + (self._rope_x_used[slot],)
        ev1 = dv(lambda e, o=xst, i=self.P[bank].ap()[:, 0:ntok]:
                 e.tensor_copy(o, i), waits=w)
        self.pfree[bank] = (self.sDV, ev1)
        swap_tok = self.mmgrp(bank, [(self.swp_sb.ap(), xst)])
        rot = self.P[bank].ap()[:, 0:ntok]
        dv(lambda e, o=out_ap, a=xst, b=blk(0): e.tensor_mul(o, a, b),
           waits=tuple(extra))
        dv(lambda e, o=tmp, a=xst, b=blk(1): e.tensor_mul(o, a, b))
        dv(lambda e, o=out_ap, a=out_ap, b=tmp: e.tensor_add(o, a, b))
        dv(lambda e, o=tmp, a=rot, b=blk(2): e.tensor_mul(o, a, b),
           waits=((self.sPE, swap_tok),))
        dv(lambda e, o=out_ap, a=out_ap, b=tmp: e.tensor_add(o, a, b))
        dv(lambda e, o=tmp, a=rot, b=blk(3): e.tensor_mul(o, a, b))
        ev4 = dv(lambda e, o=out_ap, a=out_ap, b=tmp: e.tensor_add(o, a, b))
        self._rope_x_used[slot] = (self.sDV, ev4)
        self.pfree[bank] = (self.sDV, ev4)
        self._rope_dv_done = ev4
        return ev4

    def load16(self, dst_reg, dram, waits=()):
        v = 0
        for g in range(4):
            src = bass.AP(dram, g * 4 * 128 * C, [[C, 128], [128 * C, 4], [1, C]])
            v = self.ld(dst_reg.ap(c0=g * 4 * C, c1=(g + 1) * 4 * C), src,
                        waits=waits)
        return v

    def phase_a(self):
        A = self.arena
        ld, dv, ac = self.ld, self.dv, self.ac
        t0 = 0
        for reg, dram in ((self.ident_sb, self.ident), (self.ones_sb, self.ones),
                          (self.g1_sb, self.g1c), (self.g2_sb, self.g2c),
                          (self.g3_sb, self.g3c), (self.cT_sb, self.cT),
                          (self.biasbc_sb, self.bias_bc)):
            t0 = ld(reg.ap(p0=0, p1=dram.shape[0] if dram.shape[0] < 128 else 128,
                           c0=0, c1=dram.shape[1]), dram.ap())
        gw = bass.AP(self.gate_wT, 0, [[NEXP, 128], [128 * NEXP, FC], [1, NEXP]])
        t0 = ld(self.gatew_sb.ap(), gw)
        t0 = ld(self.eps_sb.ap(), self.epsc.ap())
        t0 = ld(self.swp_sb.ap(), self.swp.ap())

        wuq = A.take(FC * C)
        wuk = A.take(FC * C)
        wuv = A.take(FC * C)
        wot = A.take(2 * C)
        wmb = A.take(2 * 1024)
        self.clamp_t = A.take(4 * FC)

        self.load16(wuq, self.w_u_q)
        self.load16(wuk, self.w_u_k)
        vv = self.load16(wuv, self.w_u_v)
        vA = self.sLD.n
        vq = vk = vv = vA
        wa = self.stream("wa", "sync", 2)
        wm = self.stream("wm", "sync", 2)
        tsil = ac(lambda e, o=self.siluc.ap(), i=self.cT_sb.ap():
                  e.activation(o, i, AF.Silu), waits=((self.sLD, vA),))

        for mc in range(CCH):
            bank = mc % 2
            parts = [(wuq.ap(c0=kc * C + mc * 128, c1=kc * C + mc * 128 + 128),
                      wuk.ap(c0=kc * C, c1=(kc + 1) * C)) for kc in range(FC)]
            tok = self.mmgrp(bank, parts, waits=((self.sLD, max(vq, vk)),))
            dv(lambda e, o=self.wqe.ap(c0=mc * C, c1=(mc + 1) * C),
               i=self.P[bank].ap(): e.tensor_scalar(o, i, SMX_HI, None, ALU.mult),
               waits=((self.sPE, tok),))
            ev = dv(lambda e, o=self.wqe.ap(c0=mc * C, c1=(mc + 1) * C),
                    i=self.P[bank].ap(), oo=self.wqe.ap(c0=mc * C, c1=(mc + 1) * C):
                    e.scalar_tensor_tensor(oo, i, SMX_LO, oo, ALU.mult, ALU.add),
                    waits=((self.sPE, tok),))
            self.pfree[bank] = (self.sDV, ev)

        wot_used = [0, 0]
        for ncol in range(4):
            for kc in range(FC):
                slot = kc % 2
                wt = wot.ap(c0=slot * C, c1=(slot + 1) * C)
                lw = wa.batch([(wt, self.w_oT.ap()[kc * 128:(kc + 1) * 128,
                                                   ncol * C:(ncol + 1) * C])],
                              waits=((self.sPE, wot_used[slot]),))
                for mc in range(CCH):
                    bank = 4 + mc
                    w = (lw,) + (self.puse(bank) if kc == 0 else ())
                    v = self.E("tensor", lambda e, o=self.P[bank].ap(),
                               l=wuv.ap(c0=kc * C + mc * 128,
                                        c1=kc * C + mc * 128 + 128),
                               r=wt, a=(kc == 0), b=(kc == FC - 1):
                               e.matmul(o, l, r, start=a, stop=b),
                               w, inc=(self.sPE if mc == CCH - 1 else None))
                    if mc == CCH - 1:
                        wot_used[slot] = v
            for mc in range(CCH):
                bank = 4 + mc
                ev = dv(lambda e, o=self.awo.ap(c0=mc * E + ncol * C,
                                                c1=mc * E + (ncol + 1) * C),
                        i=self.P[bank].ap(): e.tensor_copy(o, i),
                        waits=((self.sPE, wot_used[(FC - 1) % 2]),))
                self.pfree[bank] = (self.sDV, ev)

        wm_used = [0, 0]
        for mg in range(12):
            for kc in range(FC):
                slot = kc % 2
                wb = wmb.ap(c0=slot * 1024, c1=(slot + 1) * 1024)
                lw = wm.batch([(wb, self.wmod.ap()[kc * 128:(kc + 1) * 128,
                                                   mg * 1024:(mg + 1) * 1024])],
                              waits=((self.sPE, wm_used[slot]),))
                for j in range(8):
                    wj = (lw, (self.sAC, tsil)) + \
                         (self.puse(j) if kc == 0 else ())
                    v = self.E("tensor",
                               lambda e, o=self.P[j].ap()[:, 0:1],
                               l=wb[:, j * 128:(j + 1) * 128],
                               r=self.siluc.ap(c0=kc, c1=kc + 1),
                               a=(kc == 0), b=(kc == FC - 1):
                               e.matmul(o, l, r, start=a, stop=b),
                               wj, inc=(self.sPE if j == 7 else None))
                    if j == 7:
                        wm_used[slot] = v
            grp_end = self.sPE.n
            for j in range(8):
                m = mg * 8 + j
                ev = dv(lambda e, o=self.mod_sb.ap(c0=m, c1=m + 1),
                        i=self.P[j].ap()[:, 0:1]: e.tensor_copy(o, i),
                        waits=((self.sPE, grp_end),))
                self.pfree[j] = (self.sDV, ev)

        mready = self.sDV.n
        dv(lambda e, o=self.ca1_sb.ap(), i=self.mod_sb.ap(c0=16, c1=32):
           e.tensor_scalar_add(o, i, 1.0), waits=((self.sDV, mready),))
        dv(lambda e, o=self.cb1_sb.ap(), i=self.mod_sb.ap(c0=64, c1=80):
           e.tensor_scalar_add(o, i, 1.0))
        dv(lambda e, o=self.sa_sb.ap(), i=self.mod_sb.ap(c0=0, c1=16):
           e.tensor_copy(o, i))
        dv(lambda e, o=self.sb_sb.ap(), i=self.mod_sb.ap(c0=48, c1=64):
           e.tensor_copy(o, i))

        def clamp(dst_reg, src_ap, extra_mul=None):
            ct = self.clamp_t
            a_ = ct.ap(c0=0, c1=16)
            b_ = ct.ap(c0=16, c1=32)
            m_ = ct.ap(c0=32, c1=48)
            dv(lambda e, o=a_, i=src_ap: e.tensor_scalar_max(o, i, 0.25))
            dv(lambda e, o=b_, i=src_ap: e.tensor_scalar_min(o, i, -0.25))
            dv(lambda e, o=m_, i=src_ap: e.tensor_scalar(o, i, 0.0, None, ALU.is_ge))
            dv(lambda e, o=a_, x=a_, y=b_: e.tensor_sub(o, x, y))
            dv(lambda e, o=a_, x=m_, y=a_: e.tensor_mul(o, x, y))
            dv(lambda e, o=dst_reg.ap(), x=a_, y=b_: e.tensor_add(o, x, y))
            if extra_mul is not None:
                dv(lambda e, o=dst_reg.ap(), i=dst_reg.ap(), g=extra_mul:
                   e.tensor_mul(o, i, g))

        clamp(self.gg_sb, self.mod_sb.ap(c0=32, c1=48), extra_mul=self.g3_sb.ap())
        clamp(self.gbc_sb, self.mod_sb.ap(c0=80, c1=96))

    def phase_b(self):
        A = self.arena
        ld, st, dv, ac = self.ld, self.st, self.dv, self.ac

        A.reset(self.base_keep)
        hbuf = A.take(FC * TOK)
        wdkv = A.take(FC * C)
        ckvp = A.take(CCH * TOK)
        ckv = A.take(CCH * TOK)
        vstg = A.take(CCH * TOK)
        krst = A.take(4 * TOK)
        wkrb = A.take(2 * TOK)
        csbuf = A.take(4 * TOK)
        self.ropet = A.take(2 * TOK)
        self.ropetmp = A.take(TOK)

        self.load16(wdkv, self.w_d_kvT)
        vdkv = self.sLD.n
        cst = self.stream("csk", "sync", 1)
        hb = self.stream("hb", "sync", 1)
        wk = self.stream("wk", "sync", 2)
        sk = self.stream("sk", "gpsimd", 2)
        sv = self.stream("sv", "gpsimd", 2)
        sr = self.stream("sr", "gpsimd", 2)

        h_used_prev = 0
        sk_prev = None
        wkr_used = [0, 0]
        kr_slot_store = [None] * 4
        vstg_store = [None] * 4
        for ci in range(self.NKV):
            cs = ci * TOK
            vx = hb.batch(
                [(hbuf.ap(c0=fc * TOK, c1=(fc + 1) * TOK),
                  self.xT_full.ap()[fc * 128:(fc + 1) * 128, cs:cs + TOK])
                 for fc in range(FC)],
                waits=((self.sPE, h_used_prev),))
            vcs = cst.batch(
                [(csbuf.ap(c0=b4 * TOK, c1=(b4 + 1) * TOK),
                  self.cs_k.ap()[:, b4 * self.Sf + cs:b4 * self.Sf + cs + TOK])
                 for b4 in range(4)],
                waits=((self.sDV, self._rope_dv_done),))
            ready = (vx,)
            h_ready = self.ln_mod(hbuf, TOK, self.ca1_sb, self.sa_sb, ready=ready)

            ckv_ev = 0
            for cc in range(CCH):
                bank = 4 + (cc % 2)
                parts = [(wdkv.ap(c0=kc * C + cc * 128, c1=kc * C + cc * 128 + 128),
                          hbuf.ap(c0=kc * TOK, c1=(kc + 1) * TOK))
                         for kc in range(FC)]
                tok = self.mmgrp(bank, parts,
                                 waits=((self.sLD, vdkv), (self.sDV, h_ready)))
                ckv_ev = dv(lambda e, o=ckvp.ap(c0=cc * TOK, c1=(cc + 1) * TOK),
                            i=self.P[bank].ap(): e.tensor_copy(o, i),
                            waits=((self.sPE, tok),))
                self.pfree[bank] = (self.sDV, ckv_ev)
            src = [ckvp.ap(c0=i * TOK, c1=(i + 1) * TOK) for i in range(CCH)]
            dst = [ckv.ap(c0=i * TOK, c1=(i + 1) * TOK) for i in range(CCH)]
            ckv_done = self.rms_apply(src, dst, CCH, TOK, self.g2_sb, C, 6, 7,
                                      ready=((self.sDV, ckv_ev),) +
                                            ((sk_prev,) if sk_prev else ()))
            sk_prev = sk.batch(
                [(self.Ksc.ap()[cc * 128:(cc + 1) * 128, cs:cs + TOK],
                  ckv.ap(c0=cc * TOK, c1=(cc + 1) * TOK)) for cc in range(CCH)],
                waits=((self.sAC, ckv_done),))
            for ts in range(CCH):
                for cc in range(CCH):
                    bank = 6 + ((ts * CCH + cc) % 2)
                    tok = self.transp(bank,
                                      ckv.ap(c0=cc * TOK + ts * 128,
                                             c1=cc * TOK + ts * 128 + 128),
                                      self.ident_sb.ap(),
                                      waits=((self.sAC, ckv_done),))
                    dv(lambda e, o=vstg.ap(c0=ts * C + cc * 128,
                                           c1=ts * C + (cc + 1) * 128),
                       i=self.P[bank].ap()[:, 0:128]: e.tensor_copy(o, i),
                       waits=((self.sPE, tok),) +
                             ((vstg_store[ts],) if vstg_store[ts] else ()))
                    self.pfree[bank] = (self.sDV, self.sDV.n)
                vstg_store[ts] = sv.batch(
                    [(self.Vsc.ap()[cs + ts * 128:cs + (ts + 1) * 128, 0:C],
                      vstg.ap(c0=ts * C, c1=(ts + 1) * C))],
                    waits=((self.sDV, self.sDV.n),))
            for mg in range(4):
                for kc in range(FC):
                    slot = kc % 2
                    wb = wkrb.ap(c0=slot * TOK, c1=(slot + 1) * TOK)
                    lw = wk.batch(
                        [(wb, self.w_kr_pT.ap()[kc * 128:(kc + 1) * 128,
                                                mg * 512:(mg + 1) * 512])],
                        waits=((self.sPE, wkr_used[slot]),))
                    for j in range(4):
                        bank = 4 + j
                        wj = (lw,) + (self.puse(bank) if kc == 0 else ())
                        v = self.E("tensor",
                                   lambda e, o=self.P[bank].ap(),
                                   l=wb[:, j * 128:(j + 1) * 128],
                                   r=hbuf.ap(c0=kc * TOK, c1=(kc + 1) * TOK),
                                   a=(kc == 0), b=(kc == FC - 1):
                                   e.matmul(o, l, r, start=a, stop=b),
                                   wj, inc=(self.sPE if j == 3 else None))
                        if j == 3:
                            wkr_used[slot] = v
                grp_end = self.sPE.n
                for j in range(4):
                    bank = 4 + j
                    hd = mg * 4 + j
                    ks = krst.ap(c0=j * TOK, c1=(j + 1) * TOK)
                    ev = self._rope(bank, ks, csbuf, TOK, grp_end,
                                    extra=((kr_slot_store[j],)
                                           if kr_slot_store[j] else ()) + (vcs,))
                    kr_slot_store[j] = sr.batch(
                        [(self.Ksc.ap()[C + hd * 128:C + (hd + 1) * 128,
                                        cs:cs + TOK], ks)],
                        waits=((self.sDV, ev),))
            h_used_prev = self.sPE.n

        self.settle()
        A.reset(self.base_keep)
        hbuf = A.take(FC * TOK)
        wdq = A.take(FC * C)
        wqr = A.take(CCH * E)
        tT = A.take(CCH * TOK)
        cq = A.take(CCH * TOK)
        qst = A.take(4 * TOK)
        csbuf = A.take(4 * TOK)
        self.ropet = A.take(2 * TOK)
        self.ropetmp = A.take(TOK)

        self.load16(wdq, self.w_d_qT)
        for g in range(4):
            src = bass.AP(self.w_qr_pT, g * 128 * E, [[E, 128], [1, E]])
            ld(wqr.ap(c0=g * E, c1=(g + 1) * E), src)
        vdq = vqr = self.sLD.n
        cst = self.stream("csq", "sync", 1)
        hb = self.stream("hq", "sync", 1)
        sq = self.stream("sq", "gpsimd", 2)

        h_used_prev = 0
        q_slot_store = [None] * 4
        for ci in range(self.NQ):
            cs = ci * TOK
            vx = hb.batch(
                [(hbuf.ap(c0=fc * TOK, c1=(fc + 1) * TOK),
                  self.xT_own.ap()[fc * 128:(fc + 1) * 128, cs:cs + TOK])
                 for fc in range(FC)],
                waits=((self.sPE, h_used_prev),))
            vcs = cst.batch(
                [(csbuf.ap(c0=b4 * TOK, c1=(b4 + 1) * TOK),
                  self.cs_q.ap()[:, b4 * self.So + cs:b4 * self.So + cs + TOK])
                 for b4 in range(4)],
                waits=((self.sDV, self._rope_dv_done),))
            ready = (vx,)
            h_ready = self.ln_mod(hbuf, TOK, self.ca1_sb, self.sa_sb, ready=ready)
            t_ev = 0
            for cc in range(CCH):
                bank = 4 + (cc % 2)
                parts = [(wdq.ap(c0=kc * C + cc * 128, c1=kc * C + cc * 128 + 128),
                          hbuf.ap(c0=kc * TOK, c1=(kc + 1) * TOK))
                         for kc in range(FC)]
                tok = self.mmgrp(bank, parts,
                                 waits=((self.sLD, vdq), (self.sDV, h_ready)))
                t_ev = dv(lambda e, o=tT.ap(c0=cc * TOK, c1=(cc + 1) * TOK),
                          i=self.P[bank].ap(): e.tensor_copy(o, i),
                          waits=((self.sPE, tok),))
                self.pfree[bank] = (self.sDV, t_ev)
            for mc in range(CCH):
                bank = 4 + (mc % 2)
                parts = [(self.wqe.ap(c0=kc * C + mc * 128,
                                      c1=kc * C + mc * 128 + 128),
                          tT.ap(c0=kc * TOK, c1=(kc + 1) * TOK))
                         for kc in range(CCH)]
                tok = self.mmgrp(bank, parts, waits=((self.sDV, t_ev),))
                slot = mc % 4
                ev = dv(lambda e, o=qst.ap(c0=slot * TOK, c1=(slot + 1) * TOK),
                        i=self.P[bank].ap(): e.tensor_copy(o, i),
                        waits=((self.sPE, tok),) +
                              ((q_slot_store[slot],) if q_slot_store[slot] else ()))
                self.pfree[bank] = (self.sDV, ev)
                q_slot_store[slot] = sq.batch(
                    [(self.Qsc.ap()[mc * 128:(mc + 1) * 128, cs:cs + TOK],
                      qst.ap(c0=slot * TOK, c1=(slot + 1) * TOK))],
                    waits=((self.sDV, ev),))
            src = [tT.ap(c0=i * TOK, c1=(i + 1) * TOK) for i in range(CCH)]
            dst = [cq.ap(c0=i * TOK, c1=(i + 1) * TOK) for i in range(CCH)]
            cq_done = self.rms_apply(src, dst, CCH, TOK, self.g1_sb, C, 6, 7,
                                     ready=((self.sDV, t_ev),))
            for hd in range(FC):
                bank = 4 + (hd % 2)
                parts = [(wqr.ap(c0=kc * E + hd * 128, c1=kc * E + hd * 128 + 128),
                          cq.ap(c0=kc * TOK, c1=(kc + 1) * TOK))
                         for kc in range(CCH)]
                tok = self.mmgrp(bank, parts,
                                 waits=((self.sLD, vqr), (self.sAC, cq_done)))
                slot = hd % 4
                ks = qst.ap(c0=slot * TOK, c1=(slot + 1) * TOK)
                ev = self._rope(bank, ks, csbuf, TOK, tok,
                                extra=((q_slot_store[slot],)
                                       if q_slot_store[slot] else ()) + (vcs,))
                q_slot_store[slot] = sq.batch(
                    [(self.Qsc.ap()[C + hd * 128:C + (hd + 1) * 128, cs:cs + TOK],
                      ks)], waits=((self.sDV, ev),))
            h_used_prev = self.sPE.n

    def phase_c(self):
        A = self.arena
        ld, st, dv, ac = self.ld, self.st, self.dv, self.ac
        self.settle()
        A.reset(self.base_keep)
        qbuf = A.take(QC * TOK)
        kbuf = A.take(2 * QC * TOK)
        vbuf = A.take(CCH * TOK)
        abuf = A.take(4 * TOK)
        atb = A.take(4 * TOK)
        obuf = A.take(4 * TOK)
        ubuf = Reg(self.sb, kbuf.c0, FC * TOK)
        xbuf = Reg(self.sb, kbuf.c0 + FC * TOK, FC * TOK)
        otb = Reg(self.sb, kbuf.c0 + 2 * FC * TOK, CCH * TOK)
        assert 2 * FC * TOK + CCH * TOK <= 2 * QC * TOK

        sm = self.small
        m_run = sm.ap(c0=0, c1=4)
        l_run = sm.ap(c0=4, c1=8)
        fcol = sm.ap(c0=8, c1=12)
        mold = sm.ap(c0=12, c1=16)
        mx = sm.ap(c0=16, c1=20)
        ps_ = sm.ap(c0=20, c1=24)
        negm = sm.ap(c0=24, c1=25)

        NKB = self.Sf // TOK
        qst_ = self.stream("cq", "sync", 1)
        kst = self.stream("ck", "sync", 2)
        vst = self.stream("cv", "sync", 2)
        xst = self.stream("cx", "sync", 1)
        sx = self.stream("sx", "gpsimd", 1)
        qq_done = None
        for qq in range(self.NQ):
            qs = qq * TOK
            wq0 = (qq_done or ())
            vq = qst_.batch(
                [(qbuf.ap(c0=t * TOK, c1=(t + 1) * TOK),
                  self.Qsc.ap()[t * 128:(t + 1) * 128, qs:qs + TOK])
                 for t in range(QC)], waits=wq0)
            k_used = [0, 0]
            v_used = 0
            for kb in range(NKB):
                ks = kb * TOK
                slot = kb % 2
                kslot = Reg(self.sb, kbuf.c0 + slot * QC * TOK, QC * TOK)
                w = ((self.sPE, k_used[slot]),) if kb >= 2 else wq0
                vk = kst.batch(
                    [(kslot.ap(c0=t * TOK, c1=(t + 1) * TOK),
                      self.Ksc.ap()[t * 128:(t + 1) * 128, ks:ks + TOK])
                     for t in range(QC)], waits=w)
                vsrc = bass.AP(self.Vsc, ks * C, [[C, 128], [128 * C, 4], [1, C]])
                wv = ((self.sPE, v_used),) if kb >= 1 else wq0
                vv = vst.batch([(vbuf.ap(), vsrc)], waits=wv)
                a_ready = []
                for qb in range(4):
                    bank = qb % 2
                    parts = [(qbuf.ap(c0=t * TOK + qb * 128,
                                      c1=t * TOK + qb * 128 + 128),
                              kslot.ap(c0=t * TOK, c1=(t + 1) * TOK))
                             for t in range(QC)]
                    tok = self.mmgrp(bank, parts, waits=(vq, vk))
                    if qb == 3:
                        k_used[slot] = self.sPE.n
                    q1 = lambda apx, qb=qb: apx[:, qb:qb + 1]
                    dv(lambda e, o=q1(mx), i=self.P[bank].ap():
                       e.reduce_max(o, i, axis=AX.X), waits=((self.sPE, tok),))
                    if kb == 0:
                        dv(lambda e, o=q1(m_run), i=q1(mx): e.tensor_copy(o, i))
                    else:
                        dv(lambda e, o=q1(mold), i=q1(m_run): e.tensor_copy(o, i))
                        dv(lambda e, o=q1(m_run), a=q1(m_run), b=q1(mx):
                           e.tensor_max(o, a, b))
                        tdf = dv(lambda e, o=q1(mold), a=q1(mold), b=q1(m_run):
                                 e.tensor_sub(o, a, b))
                        ac(lambda e, o=q1(fcol), i=q1(mold):
                           e.activation(o, i, AF.Exp),
                           waits=((self.sDV, tdf),))
                    tneg = dv(lambda e, o=negm, i=q1(m_run):
                              e.tensor_scalar_mul(o, i, -1.0))
                    texp = ac(lambda e, o=abuf.ap(c0=qb * TOK, c1=(qb + 1) * TOK),
                              i=self.P[bank].ap(), b=negm:
                              e.activation(o, i, AF.Exp, bias=b),
                              waits=((self.sDV, tneg),))
                    self.pfree[bank] = (self.sAC, texp)
                    dv(lambda e, o=q1(ps_),
                       i=abuf.ap(c0=qb * TOK, c1=(qb + 1) * TOK):
                       e.reduce_sum(o, i, axis=AX.X), waits=((self.sAC, texp),))
                    if kb == 0:
                        dv(lambda e, o=q1(l_run), i=q1(ps_): e.tensor_copy(o, i))
                    else:
                        dv(lambda e, o=q1(l_run), i=q1(l_run), s=q1(fcol),
                           p=q1(ps_):
                           e.scalar_tensor_tensor(o, i, s, p, ALU.mult, ALU.add),
                           waits=((self.sAC, self.sAC.n),))
                    for kc in range(4):
                        tb = 2 + (kc % 2)
                        tok2 = self.transp(tb,
                                           abuf.ap(c0=qb * TOK + kc * 128,
                                                   c1=qb * TOK + kc * 128 + 128),
                                           self.ident_sb.ap(),
                                           waits=((self.sAC, texp),))
                        ev = dv(lambda e,
                                o=atb.ap(c0=kc * TOK + qb * 128,
                                         c1=kc * TOK + (qb + 1) * 128),
                                i=self.P[tb].ap()[:, 0:128]: e.tensor_copy(o, i),
                                waits=((self.sPE, tok2),))
                        self.pfree[tb] = (self.sDV, ev)
                    a_ready.append(self.sDV.n)
                for qb in range(4):
                    bank = 4 + (qb % 2)
                    parts = [(atb.ap(c0=kc * TOK + qb * 128,
                                     c1=kc * TOK + (qb + 1) * 128),
                              vbuf.ap(c0=kc * TOK, c1=(kc + 1) * TOK))
                             for kc in range(4)]
                    tok = self.mmgrp(bank, parts,
                                     waits=((self.sDV, a_ready[qb]), vv))
                    if qb == 3:
                        v_used = self.sPE.n
                    ob = obuf.ap(c0=qb * TOK, c1=(qb + 1) * TOK)
                    if kb == 0:
                        ev = dv(lambda e, o=ob, i=self.P[bank].ap():
                                e.tensor_copy(o, i), waits=((self.sPE, tok),))
                    else:
                        ev = dv(lambda e, o=ob, i=ob, s=fcol[:, qb:qb + 1],
                                p=self.P[bank].ap():
                                e.scalar_tensor_tensor(o, i, s, p, ALU.mult,
                                                       ALU.add),
                                waits=((self.sPE, tok),))
                    self.pfree[bank] = (self.sDV, ev)
            scores_pe_done = self.sPE.n
            rl = sm.ap(c0=28, c1=32)
            dv(lambda e, o=rl, i=l_run: e.reciprocal(o, i))
            for qb in range(4):
                ob = obuf.ap(c0=qb * TOK, c1=(qb + 1) * TOK)
                dv(lambda e, o=ob, i=ob, s=rl[:, qb:qb + 1]:
                   e.tensor_scalar(o, i, s, None, ALU.mult))
            o_done = self.sDV.n
            for qb in range(4):
                for cc in range(CCH):
                    tb = 2 + ((qb * CCH + cc) % 2)
                    tok = self.transp(tb, obuf.ap(c0=qb * TOK + cc * 128,
                                                  c1=qb * TOK + cc * 128 + 128),
                                      self.ident_sb.ap(),
                                      waits=((self.sDV, o_done),))
                    ev = dv(lambda e, o=otb.ap(c0=cc * TOK + qb * 128,
                                               c1=cc * TOK + (qb + 1) * 128),
                            i=self.P[tb].ap()[:, 0:128]: e.tensor_copy(o, i),
                            waits=((self.sPE, tok),))
                    self.pfree[tb] = (self.sDV, ev)
            ot_done = self.sDV.n
            u_ev = 0
            for mc in range(FC):
                bank = 4 + (mc % 2)
                parts = [(self.awo.ap(c0=cc * E + mc * 128,
                                      c1=cc * E + mc * 128 + 128),
                          otb.ap(c0=cc * TOK, c1=(cc + 1) * TOK))
                         for cc in range(CCH)]
                tok = self.mmgrp(bank, parts, waits=((self.sDV, ot_done),))
                u_ev = dv(lambda e, o=ubuf.ap(c0=mc * TOK, c1=(mc + 1) * TOK),
                          i=self.P[bank].ap(): e.tensor_copy(o, i),
                          waits=((self.sPE, tok),))
                self.pfree[bank] = (self.sDV, u_ev)
            for i in range(FC):
                slot = i % 2
                tt, sq = self.sq_tmp(slot, ubuf.ap(c0=i * TOK, c1=(i + 1) * TOK),
                                     TOK, extra=((self.sDV, u_ev),) if i < 2 else ())
                w = self.puse(0) if i == 0 else ()
                v = self.E("tensor", lambda e, o=self.P[0].ap()[0:1, :],
                           l=self.ones_sb.ap(c0=0, c1=1), r=tt, a=(i == 0),
                           b=(i == FC - 1): e.matmul(o, l, r, start=a, stop=b),
                           w + ((self.sAC, sq),), inc=self.sPE)
                self._tmp_used[slot] = (self.sPE, v)
            rrow = self.stats.ap(p0=0, p1=1, c0=TOK, c1=2 * TOK)
            trs = ac(lambda e, o=rrow, i=self.P[0].ap()[0:1, :]:
                     e.activation(o, i, AF.Sqrt, scale=1.0 / E,
                                  bias=self.eps_sb.ap(p0=0, p1=1, c0=1, c1=2)),
                     waits=((self.sPE, self.sPE.n),))
            trc = dv(lambda e, o=rrow, i=rrow: e.reciprocal(o, i),
                     waits=((self.sAC, trs),))
            self.pfree[0] = (self.sAC, trs)
            tR = self.mmgrp(1, [(self.ones_sb.ap(p0=0, p1=1), rrow)],
                            waits=((self.sDV, trc),))
            vx = xst.batch(
                [(xbuf.ap(c0=fc * TOK, c1=(fc + 1) * TOK),
                  self.xT_own.ap()[fc * 128:(fc + 1) * 128, qs:qs + TOK])
                 for fc in range(FC)],
                waits=((self.sPE, scores_pe_done),))
            last = 0
            for fc in range(FC):
                u = ubuf.ap(c0=fc * TOK, c1=(fc + 1) * TOK)
                x = xbuf.ap(c0=fc * TOK, c1=(fc + 1) * TOK)
                dv(lambda e, o=u, i=u, r=self.P[1].ap(): e.tensor_mul(o, i, r),
                   waits=((self.sPE, tR), vx))
                last = dv(lambda e, o=x, i=u, s=self.gg_sb.ap(c0=fc, c1=fc + 1),
                          xx=x: e.scalar_tensor_tensor(o, i, s, xx, ALU.mult,
                                                       ALU.add))
            sxt = sx.batch(
                [(self.XN.ap()[fc * 128:(fc + 1) * 128, qs:qs + TOK],
                  xbuf.ap(c0=fc * TOK, c1=(fc + 1) * TOK)) for fc in range(FC)],
                waits=((self.sDV, last),))
            self.pfree[1] = (self.sDV, last)
            qq_done = ((self.sPE, self.sPE.n), (self.sDV, self.sDV.n),
                       (self.sAC, self.sAC.n), sxt)

    def phase_d(self):
        A = self.arena
        ld, st, dv, ac = self.ld, self.st, self.dv, self.ac
        self.settle()
        A.reset(self.hdr_keep)
        xn = A.take(FC * TOK)
        h2 = A.take(FC * TOK)
        xs = A.take(FC * TOK)
        racc = A.take(FC * TOK)
        zbuf = A.take(HCH * TOK)
        w13 = A.take(2 * 2 * FC * 128)
        w2b = A.take(2 * HCH * 128)
        self.sel8_sb = A.take(NEXP * 128)
        wsb = A.take(TOK)
        vsel8 = ld(self.sel8_sb.ap(p0=0, p1=NEXP), self.sel8.ap())
        self.E("tensor", None, waits=((self.sLD, vsel8),))
        vt = A.take(4 * NEXP)
        sct = A.take(4 * NEXP)
        wt_ = A.take(4 * NEXP)
        g2_ = A.take(96)

        xnst = self.stream("dx", "sync", 1)
        w13st = self.stream("w13", "sync", 2)
        w2st = self.stream("w2", "sync", 2)
        so = self.stream("so", "gpsimd", 1)
        w13_used = [0, 0]
        w2_used = [0, 0]
        xn_prev = None
        for ci in range(self.NQ):
            cs = ci * TOK
            vx = xnst.batch(
                [(xn.ap(c0=fc * TOK, c1=(fc + 1) * TOK),
                  self.XN.ap()[fc * 128:(fc + 1) * 128, cs:cs + TOK])
                 for fc in range(FC)],
                waits=((xn_prev,) if xn_prev else ()))
            cp_done = 0
            for fc in range(FC):
                cp_done = dv(lambda e, o=h2.ap(c0=fc * TOK, c1=(fc + 1) * TOK),
                             i=xn.ap(c0=fc * TOK, c1=(fc + 1) * TOK):
                             e.tensor_copy(o, i), waits=(vx,))
            h2_ready = self.ln_mod(h2, TOK, self.cb1_sb, self.sb_sb,
                                   ready=((self.sDV, cp_done),))

            parts = [(self.gatew_sb.ap(c0=kc * NEXP, c1=(kc + 1) * NEXP),
                      h2.ap(c0=kc * TOK, c1=(kc + 1) * TOK)) for kc in range(FC)]
            tok = self.mmgrp(0, parts, waits=((self.sDV, h2_ready),))
            tsg = ac(lambda e, o=self.grnd_f.ap()[0:NEXP, 0:TOK],
                     i=self.P[0].ap()[0:NEXP, :]:
                     e.activation(o, i, AF.Sigmoid), waits=((self.sPE, tok),))
            self.pfree[0] = (self.sAC, tsg)
            for ts in range(4):
                tb = 2 + (ts % 2)
                tok2 = self.transp(tb, self.grnd_f.ap()[0:NEXP, ts * 128:
                                                        (ts + 1) * 128],
                                   self.ident_sb.ap(p0=0, p1=NEXP, c0=0, c1=NEXP),
                                   waits=((self.sAC, tsg),))
                sc_t = sct.ap(c0=ts * NEXP, c1=(ts + 1) * NEXP)
                v_t = vt.ap(c0=ts * NEXP, c1=(ts + 1) * NEXP)
                dv(lambda e, o=sc_t, i=self.P[tb].ap()[:, 0:NEXP]:
                   e.tensor_copy(o, i), waits=((self.sPE, tok2),))
                ev = dv(lambda e, o=v_t, a=self.P[tb].ap()[:, 0:NEXP],
                        b=self.biasbc_sb.ap(): e.tensor_add(o, a, b))
                self.pfree[tb] = (self.sDV, ev)
                self._top2(v_t, sc_t, wt_.ap(c0=ts * NEXP, c1=(ts + 1) * NEXP), g2_)
            for ts in range(4):
                tb = 2 + (ts % 2)
                tok2 = self.transp(tb, wt_.ap(c0=ts * NEXP, c1=(ts + 1) * NEXP),
                                   self.ident_sb.ap(),
                                   waits=((self.sDV, self.sDV.n),))
                ev = dv(lambda e, o=wsb.ap(p0=0, p1=NEXP, c0=ts * 128,
                                           c1=(ts + 1) * 128),
                        i=self.P[tb].ap()[0:NEXP, 0:128]: e.tensor_copy(o, i),
                        waits=((self.sPE, tok2),))
                self.pfree[tb] = (self.sDV, ev)
            w_ready = self.sDV.n

            for ei in range(NEXP + 1):
                e_ = ei - 1
                if ei == 0:
                    xsrc = h2
                    xs_tok = (self.sDV, h2_ready)
                else:
                    bb = 6 + (ei % 2)
                    tok = self.mmgrp(bb, [(self.sel8_sb.ap(p0=0, p1=NEXP,
                                                             c0=e_ * 128,
                                                             c1=(e_ + 1) * 128),
                                           wsb.ap(p0=0, p1=NEXP))],
                                     waits=((self.sDV, w_ready),))
                    lastx = 0
                    for fc in range(FC):
                        lastx = dv(lambda e, o=xs.ap(c0=fc * TOK, c1=(fc + 1) * TOK),
                                   a=h2.ap(c0=fc * TOK, c1=(fc + 1) * TOK),
                                   b=self.P[bb].ap(): e.tensor_mul(o, a, b),
                                   waits=((self.sPE, tok),))
                    self.pfree[bb] = (self.sDV, lastx)
                    xsrc = xs
                    xs_tok = (self.sDV, lastx)
                z_ev = 0
                for hc in range(HCH):
                    slot = hc % 2
                    wreg = Reg(self.sb, w13.c0 + slot * 2 * FC * 128, 2 * FC * 128)
                    if ei == 0:
                        o1, o3 = hc * 128 * E, hc * 128 * E
                        d1, d3 = self.sw1H, self.sw3H
                    else:
                        o1 = o3 = (e_ * HCH + hc) * 128 * E
                        d1, d3 = self.w1H, self.w3H
                    s1 = bass.AP(d1, o1, [[E, 128], [1, E]])
                    s3 = bass.AP(d3, o3, [[E, 128], [1, E]])
                    l3 = w13st.batch(
                        [(wreg.ap(c0=0, c1=FC * 128), s1),
                         (wreg.ap(c0=FC * 128, c1=2 * FC * 128), s3)],
                        waits=((self.sPE, w13_used[slot]),))
                    pb1, pb3 = (0, 1) if slot == 0 else (2, 3)
                    parts1 = [(wreg.ap(c0=kc * 128, c1=(kc + 1) * 128),
                               xsrc.ap(c0=kc * TOK, c1=(kc + 1) * TOK))
                              for kc in range(FC)]
                    parts3 = [(wreg.ap(c0=FC * 128 + kc * 128,
                                       c1=FC * 128 + (kc + 1) * 128),
                               xsrc.ap(c0=kc * TOK, c1=(kc + 1) * TOK))
                              for kc in range(FC)]
                    self.mmgrp(pb1, parts1, waits=(l3, xs_tok))
                    t3 = self.mmgrp(pb3, parts3)
                    w13_used[slot] = t3
                    tt, tz1 = None, None
                    ttap = (self.tmp1 if slot == 0 else self.tmp2).ap()
                    wz = ((self._tmp_used[slot],) if self._tmp_used[slot] else ())
                    tz1 = ac(lambda e, o=ttap, i=self.P[pb1].ap():
                             e.activation(o, i, AF.Silu),
                             waits=((self.sPE, t3),) + wz)
                    z_ev = dv(lambda e, o=zbuf.ap(c0=hc * TOK, c1=(hc + 1) * TOK),
                              a=ttap, b=self.P[pb3].ap(): e.tensor_mul(o, a, b),
                              waits=((self.sAC, tz1),))
                    self._tmp_used[slot] = (self.sDV, z_ev)
                    self.pfree[pb1] = (self.sAC, tz1)
                    self.pfree[pb3] = (self.sDV, z_ev)
                for oc in range(FC):
                    slot = oc % 2
                    w2reg = Reg(self.sb, w2b.c0 + slot * HCH * 128, HCH * 128)
                    if ei == 0:
                        s2 = bass.AP(self.sw2H, oc * 128 * HID, [[HID, 128], [1, HID]])
                    else:
                        s2 = bass.AP(self.w2H, (e_ * FC + oc) * 128 * HID,
                                     [[HID, 128], [1, HID]])
                    l2 = w2st.batch([(w2reg.ap(), s2)],
                                    waits=((self.sPE, w2_used[slot]),))
                    bank = 4 + slot
                    parts = [(w2reg.ap(c0=kc * 128, c1=(kc + 1) * 128),
                              zbuf.ap(c0=kc * TOK, c1=(kc + 1) * TOK))
                             for kc in range(HCH)]
                    tok = self.mmgrp(bank, parts,
                                     waits=(l2, (self.sDV, z_ev)))
                    w2_used[slot] = tok
                    r = racc.ap(c0=oc * TOK, c1=(oc + 1) * TOK)
                    if ei == 0:
                        ev = dv(lambda e, o=r, i=self.P[bank].ap():
                                e.tensor_copy(o, i), waits=((self.sPE, tok),))
                    else:
                        ev = dv(lambda e, o=r, a=r, b=self.P[bank].ap():
                                e.tensor_add(o, a, b), waits=((self.sPE, tok),))
                    self.pfree[bank] = (self.sDV, ev)
            last = 0
            for fc in range(FC):
                r = racc.ap(c0=fc * TOK, c1=(fc + 1) * TOK)
                x = xn.ap(c0=fc * TOK, c1=(fc + 1) * TOK)
                last = dv(lambda e, o=x, i=r, s=self.gbc_sb.ap(c0=fc, c1=fc + 1),
                          xx=x: e.scalar_tensor_tensor(o, i, s, xx, ALU.mult,
                                                       ALU.add))
            xn_prev = so.batch(
                [(self.outT.ap()[fc * 128:(fc + 1) * 128, cs:cs + TOK],
                  xn.ap(c0=fc * TOK, c1=(fc + 1) * TOK)) for fc in range(FC)],
                waits=((self.sDV, last),))

    def _top2(self, v_ap, sc_ap, w_ap, g2_):
        dv = self.dv
        a = lambda c0, c1: g2_.ap(c0=c0, c1=c1)
        vmax = a(0, 1)
        eq = a(8, 16)
        c2 = a(16, 24)
        c3 = a(24, 32)
        c4 = a(32, 40)
        iz = a(40, 48)
        m1 = a(48, 56)
        m2 = a(56, 64)
        v2 = a(64, 72)

        def first_max(vin, mout):
            dv(lambda e, o=vmax, i=vin: e.reduce_max(o, i, axis=AX.X))
            dv(lambda e, o=eq, i=vin, s=vmax:
               e.tensor_scalar(o, i, s, None, ALU.is_ge))
            dv(lambda e, o=c2[:, 0:1], i=eq[:, 0:1]: e.tensor_copy(o, i))
            dv(lambda e, o=c2[:, 1:8], x=eq[:, 1:8], y=eq[:, 0:7]:
               e.tensor_add(o, x, y))
            dv(lambda e, o=c3[:, 0:2], i=c2[:, 0:2]: e.tensor_copy(o, i))
            dv(lambda e, o=c3[:, 2:8], x=c2[:, 2:8], y=c2[:, 0:6]:
               e.tensor_add(o, x, y))
            dv(lambda e, o=c4[:, 0:4], i=c3[:, 0:4]: e.tensor_copy(o, i))
            dv(lambda e, o=c4[:, 4:8], x=c3[:, 4:8], y=c3[:, 0:4]:
               e.tensor_add(o, x, y))
            dv(lambda e, o=c4, x=c4, y=eq: e.tensor_sub(o, x, y))
            dv(lambda e, o=iz, i=c4: e.tensor_scalar(o, i, 0.5, None, ALU.is_le))
            dv(lambda e, o=mout, x=eq, y=iz: e.tensor_mul(o, x, y))

        first_max(v_ap, m1)
        dv(lambda e, o=v2, i=m1, x=v_ap:
           e.scalar_tensor_tensor(o, i, -1e5, x, ALU.mult, ALU.add))
        first_max(v2, m2)
        dv(lambda e, o=m1, x=m1, y=m2: e.tensor_add(o, x, y))
        dv(lambda e, o=w_ap, x=m1, y=sc_ap: e.tensor_mul(o, x, y))

    def build(self):
        self.phase_a()
        self.settle()
        self.phase_b()
        self.phase_c()
        self.phase_d()
        self.settle()

        nc = self.nc
        with nc.Block() as blk:
            for eng, attr in (("sync", blk.sync), ("gpsimd", blk.gpsimd),
                              ("tensor", blk.tensor), ("vector", blk.vector),
                              ("scalar", blk.scalar)):
                def run(e, eng=eng):
                    for f in self.q[eng]:
                        f(e)
                attr(run)
        for ctx in self._sem_ctxs:
            ctx.__exit__(None, None, None)
        nc.finalize()
        return nc



def _rope_tables(S, scale=1.0):
    D = 128
    cos = sin = None
    try:
        import jax
        import jax.numpy as jnp
        cpu = jax.devices("cpu")[0]
        with jax.default_device(cpu):
            freqs = 1.0 / 10000.0 ** (jnp.arange(0, D, 2, dtype=jnp.float32) / D)
            ang = jnp.arange(S, dtype=jnp.float32)[:, None] * freqs[None, :]
            cos = np.asarray(jax.device_get(jnp.cos(ang))).T
            sin = np.asarray(jax.device_get(jnp.sin(ang))).T
    except Exception:
        pass
    if cos is None:
        e32 = (np.arange(0, D, 2).astype(np.float32) / np.float32(D))
        freqs = np.float32(1.0) / (np.float64(10000.0)
                                   ** e32.astype(np.float64)).astype(np.float32)
        ang = (np.arange(S, dtype=np.float32)[:, None]
               * freqs[None, :]).astype(np.float32)
        cos = np.cos(ang.astype(np.float64)).astype(np.float32).T
        sin = np.sin(ang.astype(np.float64)).astype(np.float32).T
    A = np.concatenate([cos, cos], axis=0).astype(np.float64)
    B = np.concatenate([-sin, sin], axis=0).astype(np.float64)
    A *= scale
    B *= scale
    Ahi = A.astype(np.float32)
    Alo = (A - Ahi).astype(np.float32)
    Bhi = B.astype(np.float32)
    Blo = (B - Bhi).astype(np.float32)
    return np.concatenate([Ahi, Alo, Bhi, Blo], axis=1)


def _head_perm():
    p = np.empty(E, dtype=np.int64)
    for h in range(16):
        base = h * 128
        p[base:base + 64] = base + np.arange(0, 128, 2)
        p[base + 64:base + 128] = base + np.arange(1, 128, 2)
    return p


def _tile_w13(w):
    n = w.shape[0]
    wt = w.transpose(0, 2, 1).reshape(n, FC, 128, HCH, 128)
    return np.ascontiguousarray(wt.transpose(0, 3, 2, 1, 4)).reshape(n * HCH * 128, E)


def _tile_w2(w):
    n = w.shape[0]
    wt = w.transpose(0, 2, 1).reshape(n, HCH, 128, FC, 128)
    return np.ascontiguousarray(wt.transpose(0, 3, 2, 1, 4)).reshape(n * FC * 128, HID)


def prep_inputs(inputs, S_full, S_own, cores):
    cc = np.ascontiguousarray
    perm = _head_perm()
    x = np.asarray(inputs["L0_attn_h"], dtype=np.float32)
    cs_k = cc(_rope_tables(S_full))
    cs_q_full = _rope_tables(S_full, scale=1.0 / 12.0)
    swp = np.zeros((128, 128), dtype=np.float32)
    for i in range(128):
        swp[i, i ^ 64] = 1.0

    shared = {
        "wmod": cc(np.asarray(inputs["L0_wmod"], np.float32)),
        "w_d_kvT": cc(np.asarray(inputs["L0_attn_w_d_kv"], np.float32).T),
        "w_d_qT": cc(np.asarray(inputs["L0_attn_w_d_q"], np.float32).T),
        "w_u_q": cc(np.asarray(inputs["L0_attn_w_u_q"], np.float32)),
        "w_u_k": cc(np.asarray(inputs["L0_attn_w_u_k"], np.float32)),
        "w_u_v": cc(np.asarray(inputs["L0_attn_w_u_v"], np.float32)),
        "w_oT": cc(np.asarray(inputs["L0_attn_w_o"], np.float32).T),
        "w_qr_pT": cc(np.asarray(inputs["L0_attn_w_qr"], np.float32)[perm].T),
        "w_kr_pT": cc(np.asarray(inputs["L0_attn_w_kr"], np.float32)[perm].T),
        "g1c": cc(np.asarray(inputs["L0_attn_g1"], np.float32).reshape(CCH, 128).T),
        "g2c": cc(np.asarray(inputs["L0_attn_g2"], np.float32).reshape(CCH, 128).T),
        "g3c": cc(np.asarray(inputs["L0_attn_g3"], np.float32).reshape(FC, 128).T),
        "gate_wT": cc(np.asarray(inputs["L0_ffn_gate_w"], np.float32).T),
        "bias_bc": cc(np.broadcast_to(
            np.asarray(inputs["L0_ffn_expert_bias"], np.float32)[None, :],
            (128, NEXP)).copy()),
        "sw1H": _tile_w13(np.asarray(inputs["L0_ffn_sw1"], np.float32)[None]),
        "sw3H": _tile_w13(np.asarray(inputs["L0_ffn_sw3"], np.float32)[None]),
        "sw2H": _tile_w2(np.asarray(inputs["L0_ffn_sw2"], np.float32)[None]),
        "w1H": _tile_w13(np.asarray(inputs["L0_ffn_w1"], np.float32)),
        "w3H": _tile_w13(np.asarray(inputs["L0_ffn_w3"], np.float32)),
        "w2H": _tile_w2(np.asarray(inputs["L0_ffn_w2"], np.float32)),
        "cs_k": cs_k,
        "ident": np.eye(128, dtype=np.float32),
        "ones": np.ones((128, 128), dtype=np.float32),
        "epsc": np.broadcast_to(np.array([EPS_LN, EPS_RMS], np.float32)[None, :],
                                (128, 2)).copy(),
        "swp": swp,
        "sel8": np.concatenate(
            [np.where(np.arange(NEXP)[:, None] == e_,
                      np.float32(1.0), np.float32(0.0)).repeat(128, axis=1)
             for e_ in range(NEXP)], axis=1).astype(np.float32),
    }
    cmat = np.asarray(inputs["L0_c"], np.float32)
    in_maps = []
    for core in range(cores):
        b = (core // 2) % x.shape[0]
        h = core % 2
        xT = cc(x[b].T)
        own = slice(h * S_own, (h + 1) * S_own)
        m = dict(shared)
        m["xT_full"] = xT
        m["xT_own"] = cc(xT[:, own])
        m["cT"] = cc(cmat[b].reshape(FC, 128).T)
        m["cs_q"] = cc(np.concatenate(
            [cs_q_full[:, b4 * S_full + h * S_own:b4 * S_full + (h + 1) * S_own]
             for b4 in range(4)], axis=1))
        in_maps.append(m)
    return in_maps


def kernel(**inputs):
    x = np.asarray(inputs["L0_attn_h"])
    B, S_full, _ = x.shape
    n_cores = 8
    S_own = S_full * B // n_cores
    kb = KB(S_full, S_own)
    nc = kb.build()
    in_maps = prep_inputs(inputs, S_full, S_own, n_cores)
    res = run_bass_kernel_spmd(nc, in_maps, core_ids=list(range(n_cores)))
    out = np.empty((B, S_full, E), dtype=np.float32)
    for core in range(n_cores):
        b = core // 2
        h = core % 2
        out[b, h * S_own:(h + 1) * S_own, :] = \
            np.asarray(res.results[core]["outT"]).T
    return out
